# version 11
# baseline (speedup 1.0000x reference)
from contextlib import ExitStack

import numpy as np
import ml_dtypes

import concourse.bass as bass
import concourse.mybir as mybir
from concourse.bass import IndirectOffsetOnAxis
from concourse.bass_utils import run_bass_kernel_spmd

F32 = mybir.dt.float32
BF16 = mybir.dt.bfloat16
U32 = mybir.dt.uint32
AF = mybir.ActivationFunctionType
ALU = mybir.AluOpType
AX = mybir.AxisListType

NCORES = 8
D = 4096
H = 11008
NHC = H // 128
NKC = D // 128
FQ = 2176
NFC = FQ // 128
EPS = 1e-5

NF4_CODE = np.array(
    [-1.0, -0.6961928009986877, -0.5250730514526367, -0.39491748809814453,
     -0.28444138169288635, -0.18477343022823334, -0.09105003625154495, 0.0,
     0.07958029955625534, 0.16093020141124725, 0.24611230194568634,
     0.33791524171829224, 0.44070982933044434, 0.5626170039176941,
     0.7229568362236023, 1.0], dtype=np.float32)
NF4_BOUNDS = ((NF4_CODE[:-1] + NF4_CODE[1:]) * np.float32(0.5)).astype(np.float32)
NF4_DELTA = (NF4_CODE[1:] - NF4_CODE[:-1]).astype(np.float32)

ENGINES = ("sync", "vector", "scalar", "tensor", "gpsimd")


def _build_consts():
    out = {}
    dd = np.arange(D, dtype=np.int64)
    ff = np.arange(FQ, dtype=np.int64)
    dmat = dd[:, None] * ff[None, :]
    ang = 2.0 * np.pi * ((dmat % D).astype(np.float64)) / D
    re = np.cos(ang)
    im = np.sin(ang)
    re[:, 2049:] = 0.0
    im[:, 2049:] = 0.0

    def _lhsT_tiles(m):
        m4 = m.reshape(NKC, 128, NFC, 128)
        return np.ascontiguousarray(
            m4.transpose(2, 1, 0, 3).reshape(NFC, 128, NKC * 128))

    out["dftre"] = _lhsT_tiles(re).astype(ml_dtypes.bfloat16)
    out["dftim"] = _lhsT_tiles(im).astype(ml_dtypes.bfloat16)
    w = np.zeros(FQ, dtype=np.float64)
    w[0] = 1.0 / D
    w[1:2048] = 2.0 / D
    w[2048] = 1.0 / D
    tt_ = np.arange(D, dtype=np.int64)
    fm = ff[:, None] * tt_[None, :]
    cosm = w[:, None] * np.cos(2.0 * np.pi * ((fm % D).astype(np.float64)) / D)
    cosm[2049:, :] = 0.0
    out["cosm"] = np.ascontiguousarray(
        cosm.reshape(NFC, 128, D)).astype(ml_dtypes.bfloat16)
    out["identb"] = np.eye(128, dtype=ml_dtypes.bfloat16)
    out["identf"] = np.eye(128, dtype=np.float32)
    out["rowbase"] = (np.arange(128, dtype=np.float32) * 8192.0).reshape(128, 1)
    return out


class KB:

    def __init__(self, nc):
        self.nc = nc
        self.steps = {e: [] for e in ENGINES}
        self.sem = {}
        self.cnt = {}
        self.ctx = ExitStack()

    def new_sem(self, *names):
        for n in names:
            self.sem[n] = self.ctx.enter_context(self.nc.semaphore(n))
            self.cnt[n] = 0

    def wait(self, eng, name, val=None):
        v = self.cnt[name] if val is None else val
        h = self.sem[name]
        if v > 0:
            self.steps[eng].append(lambda e, h=h, v=v: e.wait_ge(h, v))
        return v

    def emit(self, eng, fn, inc=None):
        pair = None
        if inc is not None:
            nm, n = inc
            self.cnt[nm] += n
            pair = (self.sem[nm], n)

        def step(e, fn=fn, pair=pair):
            ins = fn(e)
            if pair is not None:
                ins.then_inc(pair[0], pair[1])
        self.steps[eng].append(step)
        return self.cnt[inc[0]] if inc else None

    def drain(self, eng):
        self.steps[eng].append(lambda e: e.drain())

    def finalize_block(self):
        block = self.ctx.enter_context(self.nc.Block())
        for eng in ENGINES:
            steps = self.steps[eng]
            if not steps:
                continue
            dec = getattr(block, eng)

            def section(e, steps=steps):
                for s in steps:
                    s(e)
            dec(section)


def _ln_emit(B, src_ap, st, outs, sqscr=None):
    sm = st[:, 0:1]; sq = st[:, 1:2]; m = st[:, 2:3]; msq = st[:, 3:4]
    m2 = st[:, 4:5]; var = st[:, 5:6]; vpe = st[:, 6:7]; sd = st[:, 7:8]
    r0 = st[:, 8:9]; r0s = st[:, 9:10]; tm = st[:, 10:11]; co = st[:, 11:12]
    r = st[:, 12:13]; nmr = st[:, 13:14]

    B.wait("scalar", "v")
    B.drain("scalar")
    B.emit("vector", lambda e: e.reduce_sum(out=sm, in_=src_ap, axis=AX.X),
           inc=("v", 1))
    B.drain("vector")
    scr = outs[0] if sqscr is None else sqscr
    B.emit("scalar", lambda e: e.activation(out=scr, in_=src_ap,
                                            func=AF.Square, accum_out=sq),
           inc=("a", 1))
    B.drain("scalar")
    B.wait("vector", "a")
    B.emit("vector", lambda e: e.tensor_scalar_mul(m, sm, 1.0 / 4096.0))
    B.drain("vector")
    B.emit("vector", lambda e: e.tensor_scalar_mul(msq, sq, 1.0 / 4096.0))
    B.drain("vector")
    B.emit("vector", lambda e: e.tensor_mul(m2, m, m))
    B.drain("vector")
    B.emit("vector", lambda e: e.tensor_sub(var, msq, m2))
    B.drain("vector")
    B.emit("vector", lambda e: e.tensor_scalar_add(vpe, var, float(EPS)),
           inc=("v", 1))
    B.drain("vector")
    B.wait("scalar", "v")
    B.emit("scalar", lambda e: e.activation(out=sd, in_=vpe, func=AF.Sqrt),
           inc=("a", 1))
    B.drain("scalar")
    B.wait("vector", "a")
    B.emit("vector", lambda e: e.reciprocal(out=r0, in_=sd))
    B.drain("vector")
    B.emit("vector", lambda e: e.tensor_mul(r0s, r0, r0))
    B.drain("vector")
    B.emit("vector", lambda e: e.tensor_mul(tm, r0s, vpe))
    B.drain("vector")
    B.emit("vector", lambda e: e.tensor_scalar(out=co, in0=tm, scalar1=-0.5,
                                               scalar2=1.5, op0=ALU.mult,
                                               op1=ALU.add))
    B.drain("vector")
    B.emit("vector", lambda e: e.tensor_mul(r, r0, co))
    B.drain("vector")
    B.emit("vector", lambda e: e.scalar_tensor_tensor(
        out=nmr, in0=m, scalar=-1.0, in1=r, op0=ALU.mult, op1=ALU.mult),
        inc=("v", 1))
    B.drain("vector")
    B.wait("scalar", "v")
    for i, o in enumerate(outs):
        B.emit("scalar", lambda e, o=o: e.activation(
            out=o, in_=src_ap, func=AF.Identity, bias=nmr, scale=r),
            inc=("a", 1) if i == len(outs) - 1 else None)
        B.drain("scalar")
    return B.cnt["a"]


def build_nc(T=2048, debug=False, phases="012", ncores=8):
    nc = bass.Bass("TRN2", target_bir_lowering=False, debug=False,
                   num_devices=ncores)
    NTILE = T // 512

    W1T = 88 // ncores
    W3T = NKC // ncores
    x_d = nc.dram_tensor("x", [T, D], F32, kind="ExternalInput").ap()
    w1_d = nc.dram_tensor("w1s", [W1T * 128, D], F32, kind="ExternalInput").ap()
    w2_d = nc.dram_tensor("w2s", [W1T * 128, D], F32, kind="ExternalInput").ap()
    w3_d = nc.dram_tensor("w3s", [W3T * 128, H], F32, kind="ExternalInput").ap()
    dftre_d = nc.dram_tensor("dftre", [NFC, 128, NKC * 128], BF16,
                             kind="ExternalInput").ap()
    dftim_d = nc.dram_tensor("dftim", [NFC, 128, NKC * 128], BF16,
                             kind="ExternalInput").ap()
    cosm_d = nc.dram_tensor("cosm", [NFC, 128, D], BF16,
                            kind="ExternalInput").ap()
    identb_d = nc.dram_tensor("identb", [128, 128], BF16,
                              kind="ExternalInput").ap()
    identf_d = nc.dram_tensor("identf", [128, 128], F32,
                              kind="ExternalInput").ap()
    rowb_d = nc.dram_tensor("rowbase", [128, 1], F32,
                            kind="ExternalInput").ap()
    out_d = nc.dram_tensor("out", [T, D], F32, kind="ExternalOutput").ap()

    _ccsp = "Shared" if ncores > 4 else "Local"
    c1_d = nc.dram_tensor("c1T", [W1T, 128, NKC * 128], BF16,
                          kind="Internal").ap()
    c2_d = nc.dram_tensor("c2T", [W1T, 128, NKC * 128], BF16,
                          kind="Internal").ap()
    c3_d = nc.dram_tensor("c3T", [W3T, 128, NHC * 128], BF16,
                          kind="Internal").ap()
    dq1_d = nc.dram_tensor("dq1T", [ncores * W1T, 128, NKC * 128], BF16,
                           kind="Internal", addr_space=_ccsp).ap()
    dq2_d = nc.dram_tensor("dq2T", [ncores * W1T, 128, NKC * 128], BF16,
                           kind="Internal", addr_space=_ccsp).ap()
    dq3_d = nc.dram_tensor("dq3T", [ncores * W3T, 128, NHC * 128], BF16,
                           kind="Internal", addr_space=_ccsp).ap()
    x1_d = nc.dram_tensor("x1d", [T, D], F32, kind="Internal").ap()
    ln2_d = nc.dram_tensor("ln2d", [D, T], BF16, kind="Internal").ap()
    vv_d = [nc.dram_tensor(f"vvd{i}", [128, 2 * D], F32,
                           kind="Internal").ap() for i in range(4)]
    ytm_d = nc.dram_tensor("ytmd", [4, 128, D], F32, kind="Internal").ap()

    dbg = {}
    if debug:
        def dbgt(name, shape, dtype=F32):
            dbg[name] = nc.dram_tensor("dbg_" + name, shape, dtype,
                                       kind="ExternalOutput").ap()
        dbgt("ln1", [128, D]); dbgt("c", [128, D]); dbgt("top8", [128, 8])
        dbgt("idx8", [128, 8], U32); dbgt("attn", [128, D])
        dbgt("x1", [128, D]); dbgt("dq1", [128, NKC * 128])
        dbgt("yt", [128, D])

    B = KB(nc)
    B.new_sem("v", "a", "t", "cst",
              "wl_a", "wl_b", "dqs_a", "dqs_b",
              "xi_a", "xi_b", "gv", "gx", "gp", "df_a", "df_b", "cs_a", "cs_b",
              "gs_a", "gs_b", "gs_c", "gs_d", "sx1", "sl2_a", "sl2_b",
              "l2l", "w12_a", "w12_b", "w12_c", "w2_a", "w2_b", "w2_c",
              "d3_a", "d3_b", "gy_a", "gy_b",
              "ye_a", "ye_b", "so_a", "so_b", "cc", "dbgs")
    ctx = B.ctx

    identb = ctx.enter_context(nc.sbuf_tensor("identb_s", [128, 128], BF16))
    identf = ctx.enter_context(nc.sbuf_tensor("identf_s", [128, 128], F32))
    rowb = ctx.enter_context(nc.sbuf_tensor("rowb_s", [128, 1], F32))
    B.emit("sync", lambda e: e.dma_start(out=identb[:], in_=identb_d),
           inc=("cst", 16))
    B.emit("sync", lambda e: e.dma_start(out=identf[:], in_=identf_d),
           inc=("cst", 16))
    B.emit("sync", lambda e: e.dma_start(out=rowb[:], in_=rowb_d),
           inc=("cst", 16))
    const_ld = B.cnt["cst"]
    for eng in ("vector", "scalar", "tensor", "gpsimd"):
        B.wait(eng, "cst", const_ld)

    with ExitStack() as p0ctx:
        FMAX = 5504
        wt = [p0ctx.enter_context(nc.sbuf_tensor(f"wt{i}", [128, FMAX], F32))
              for i in range(2)]
        nb_t = p0ctx.enter_context(nc.sbuf_tensor("nbuf", [128, FMAX], F32))
        acc_t = p0ctx.enter_context(nc.sbuf_tensor("acc", [128, FMAX], F32))
        pl_t = p0ctx.enter_context(nc.sbuf_tensor("plane", [128, FMAX], F32))
        am_t = p0ctx.enter_context(nc.sbuf_tensor("am", [128, 92], F32))
        eq_t = p0ctx.enter_context(nc.sbuf_tensor("eqz", [128, 92], F32))
        ag_t = p0ctx.enter_context(nc.sbuf_tensor("amg", [128, 92], F32))
        dq_t = [p0ctx.enter_context(
            nc.sbuf_tensor(f"dqb{i}", [128, FMAX], BF16)) for i in range(2)]
        accg_t = p0ctx.enter_context(
            nc.sbuf_tensor("accg", [128, FMAX], F32))
        plg_t = p0ctx.enter_context(
            nc.sbuf_tensor("plg", [128, FMAX], F32))
        stg = [p0ctx.enter_context(nc.sbuf_tensor(f"stg{i}", [128, 1024], BF16))
               for i in range(2)]
        ptr = [p0ctx.enter_context(nc.psum_tensor(f"ptr{i}", [128, 1024], BF16))
               for i in range(2)]

        jobs = []
        for hc in range(W1T):
            jobs.append((w1_d[hc * 128:(hc + 1) * 128, :], D, c1_d, hc, 0))
        for hc in range(W1T):
            jobs.append((w2_d[hc * 128:(hc + 1) * 128, :], D, c2_d, hc, 0))
        for dc in range(W3T):
            for half in range(2):
                jobs.append((w3_d[dc * 128:(dc + 1) * 128,
                                  half * 5504:(half + 1) * 5504], 5504,
                             c3_d, dc, half * 43))

        wl_sems = ["wl_a", "wl_b"]
        dqs_sems = ["dqs_a", "dqs_b"]
        v_after_div = {}
        v_comb = {}
        v_after_evac = {}
        t_tile_last = {}
        gidx = 0

        for ji, (src_ap, F, dst_ap, c0, h0) in enumerate(jobs):
            slot = ji % 2
            if ji >= 2:
                B.wait("sync", "v", v_after_div[ji - 2])
            B.emit("sync",
                   lambda e, src_ap=src_ap, F=F, slot=slot:
                   e.dma_start(out=wt[slot][:, 0:F], in_=src_ap),
                   inc=(wl_sems[slot], 16))
            wl_tot = B.cnt[wl_sems[slot]]
            nbk = F // 64
            wv = wt[slot][:, 0:F].rearrange("p (nb e) -> p nb e", e=64)
            amv = am_t[:, 0:nbk]
            amo = am_t[:, 0:nbk].rearrange("p (nb one) -> p nb one", one=1)
            B.wait("vector", wl_sems[slot], wl_tot)
            B.emit("vector", lambda e, wv=wv, amo=amo: e.tensor_reduce(
                out=amo, in_=wv, axis=AX.X, op=ALU.max,
                apply_absolute_value=True))
            B.drain("vector")
            B.emit("vector", lambda e, amv=amv, nbk=nbk: e.tensor_scalar(
                out=eq_t[:, 0:nbk], in0=amv, scalar1=0.0, scalar2=None,
                op0=ALU.is_equal))
            B.drain("vector")
            B.emit("vector", lambda e, amv=amv, nbk=nbk: e.tensor_add(
                ag_t[:, 0:nbk], amv, eq_t[:, 0:nbk]))
            B.drain("vector")
            B.emit("vector", lambda e, nbk=nbk: e.reciprocal(
                out=eq_t[:, 0:nbk], in_=ag_t[:, 0:nbk]))
            B.drain("vector")
            rgb = eq_t[:, 0:nbk].unsqueeze(2).broadcast_to([128, nbk, 64])
            agb = ag_t[:, 0:nbk].unsqueeze(2).broadcast_to([128, nbk, 64])
            nv = nb_t[:, 0:F].rearrange("p (nb e) -> p nb e", e=64)
            B.emit("vector", lambda e, wv=wv, rgb=rgb, nv=nv: e.tensor_tensor(
                out=nv, in0=wv, in1=rgb, op=ALU.mult), inc=("v", 1))
            v_after_div[ji] = B.cnt["v"]
            B.drain("vector")
            nfl = nb_t[:, 0:F]
            B.wait("gpsimd", "v", v_after_div[ji])
            if ji >= 1:
                B.wait("gpsimd", "v", v_comb[ji - 1])
            B.emit("gpsimd", lambda e, nfl=nfl, F=F: e.tensor_scalar(
                out=accg_t[:, 0:F], in0=nfl,
                scalar1=float(NF4_BOUNDS[10]), scalar2=float(NF4_DELTA[10]),
                op0=ALU.is_gt, op1=ALU.mult))
            B.drain("gpsimd")
            for b in range(11, 15):
                B.emit("gpsimd", lambda e, nfl=nfl, F=F, b=b:
                       e.tensor_scalar(
                           out=plg_t[:, 0:F], in0=nfl,
                           scalar1=float(NF4_BOUNDS[b]),
                           scalar2=float(NF4_DELTA[b]),
                           op0=ALU.is_gt, op1=ALU.mult))
                B.drain("gpsimd")
                B.emit("gpsimd", lambda e, F=F: e.tensor_add(
                    accg_t[:, 0:F], accg_t[:, 0:F], plg_t[:, 0:F]),
                    inc=("gp", 1) if b == 14 else None)
                B.drain("gpsimd")
            gp_done = B.cnt["gp"]
            B.emit("vector", lambda e, nfl=nfl, F=F: e.tensor_scalar(
                out=acc_t[:, 0:F], in0=nfl,
                scalar1=float(NF4_BOUNDS[0]), scalar2=float(NF4_DELTA[0]),
                op0=ALU.is_gt, op1=ALU.mult))
            B.drain("vector")
            for b in range(1, 10):
                B.emit("vector", lambda e, nfl=nfl, F=F, b=b: e.tensor_scalar(
                    out=pl_t[:, 0:F], in0=nfl,
                    scalar1=float(NF4_BOUNDS[b]), scalar2=float(NF4_DELTA[b]),
                    op0=ALU.is_gt, op1=ALU.mult))
                B.drain("vector")
                B.emit("vector", lambda e, F=F: e.tensor_add(
                    acc_t[:, 0:F], acc_t[:, 0:F], pl_t[:, 0:F]))
                B.drain("vector")
            B.wait("vector", "gp", gp_done)
            B.emit("vector", lambda e, F=F: e.tensor_add(
                acc_t[:, 0:F], acc_t[:, 0:F], accg_t[:, 0:F]),
                inc=("v", 1))
            B.drain("vector")
            v_comb[ji] = B.cnt["v"]
            if ji >= 2:
                B.wait("vector", "t", t_tile_last[ji - 2])
            dqv = dq_t[slot][:, 0:F].rearrange("p (nb e) -> p nb e", e=64)
            accv = acc_t[:, 0:F].rearrange("p (nb e) -> p nb e", e=64)
            B.emit("vector", lambda e, accv=accv, agb=agb, dqv=dqv:
                   e.scalar_tensor_tensor(out=dqv, in0=accv, scalar=-1.0,
                                          in1=agb, op0=ALU.add, op1=ALU.mult),
                   inc=("v", 1))
            v_dq = B.cnt["v"]
            B.drain("vector")
            ngroups = [(g * 8, 8) for g in range(F // 1024)]
            if F % 1024:
                ngroups.append(((F // 1024) * 8, (F % 1024) // 128))
            B.wait("tensor", "v", v_dq)
            for (k0, ng) in ngroups:
                par = gidx % 2
                if gidx >= 2:
                    B.wait("tensor", "v", v_after_evac[gidx - 2])
                for j in range(ng):
                    B.emit("tensor",
                           lambda e, k0=k0, j=j, par=par, slot=slot:
                           e.transpose(
                               out=ptr[par][:, j * 128:(j + 1) * 128],
                               in_=dq_t[slot][:, (k0 + j) * 128:
                                              (k0 + j + 1) * 128],
                               identity=identb[:]),
                           inc=("t", 1) if j == ng - 1 else None)
                t_grp = B.cnt["t"]
                ssl = gidx % 2
                B.wait("vector", "t", t_grp)
                B.wait("vector", dqs_sems[ssl], B.cnt[dqs_sems[ssl]])
                B.emit("vector", lambda e, ng=ng, par=par, ssl=ssl:
                       e.tensor_copy(out=stg[ssl][:, 0:ng * 128],
                                     in_=ptr[par][:, 0:ng * 128]),
                       inc=("v", 1))
                B.drain("vector")
                v_after_evac[gidx] = B.cnt["v"]
                dst = dst_ap[c0].rearrange("p (kk h) -> p kk h", h=128)[
                    :, h0 + k0:h0 + k0 + ng, :]
                B.wait("sync", "v", v_after_evac[gidx])
                B.emit("sync", lambda e, dst=dst, ng=ng, ssl=ssl: e.dma_start(
                    out=dst,
                    in_=stg[ssl][:, 0:ng * 128].rearrange(
                        "p (kk h) -> p kk h", h=128)),
                    inc=(dqs_sems[ssl], 16))
                gidx += 1
            t_tile_last[ji] = B.cnt["t"]
            if debug and ji == 0:
                B.wait("sync", "v", v_dq)
                B.emit("sync", lambda e: e.dma_start(out=dbg["dq1"],
                                                     in_=dq_t[0][:, 0:D]),
                       inc=("dbgs", 16))

        p0_dqs_a = B.cnt["dqs_a"]
        p0_dqs_b = B.cnt["dqs_b"]

    B.wait("gpsimd", "dqs_a", p0_dqs_a)
    B.wait("gpsimd", "dqs_b", p0_dqs_b)
    grp = [list(range(ncores))]
    for (cin, cout) in ((c1_d, dq1_d), (c2_d, dq2_d), (c3_d, dq3_d)):
        B.emit("gpsimd", lambda e, cin=cin, cout=cout: e.collective_compute(
            "AllGather", ALU.bypass, replica_groups=grp,
            ins=[cin.rearrange("t p c -> (t p) c")],
            outs=[cout.rearrange("t p c -> (t p) c")]),
            inc=("cc", 1))
    p0_cc = B.cnt["cc"]

    if "1" in phases:
        for eng in ENGINES:
            B.wait(eng, "dqs_a", p0_dqs_a)
            B.wait(eng, "dqs_b", p0_dqs_b)
            if debug:
                B.wait(eng, "dbgs")

        with ExitStack() as p1ctx:
            xb = p1ctx.enter_context(nc.sbuf_tensor("xb", [128, D], F32))
            castC = [p1ctx.enter_context(
                nc.sbuf_tensor(f"castC{i}", [128, D], BF16))
                for i in range(2)]
            ln1fm = p1ctx.enter_context(
                nc.sbuf_tensor("ln1fm", [128, NKC * 512], BF16))
            st_t = p1ctx.enter_context(nc.sbuf_tensor("lnst", [128, 16], F32))
            pfm = p1ctx.enter_context(
                nc.sbuf_tensor("pfm", [128, NFC * 512], BF16))
            l2st0 = p1ctx.enter_context(
                nc.sbuf_tensor("l2st0", [128, 1024], BF16))
            l2st = [l2st0, l2st0]
            top8 = [p1ctx.enter_context(nc.sbuf_tensor(
                f"top8_{i}", [128, 8], F32)) for i in range(2)]
            idx8u = [p1ctx.enter_context(nc.sbuf_tensor(
                f"idx8u_{i}", [128, 8], U32)) for i in range(2)]
            z8 = [p1ctx.enter_context(nc.sbuf_tensor(
                f"z8_{i}", [128, 8], F32)) for i in range(2)]
            e8 = [p1ctx.enter_context(nc.sbuf_tensor(
                f"e8_{i}", [128, 8], F32)) for i in range(2)]
            w8 = [p1ctx.enter_context(nc.sbuf_tensor(
                f"w8_{i}", [128, 8], F32)) for i in range(2)]
            d32 = [p1ctx.enter_context(nc.sbuf_tensor(
                f"d32_{i}", [128, 8], F32)) for i in range(2)]
            fidx = [p1ctx.enter_context(nc.sbuf_tensor(
                f"fidx_{i}", [128, 8], F32)) for i in range(2)]
            gidxu = [p1ctx.enter_context(nc.sbuf_tensor(
                f"gidxu_{i}", [128, 8], U32)) for i in range(2)]
            dfr = [p1ctx.enter_context(
                nc.sbuf_tensor(f"dfr{i}", [128, 2 * NKC * 128], BF16))
                for i in range(2)]
            sqa = p1ctx.enter_context(nc.sbuf_tensor("sqa", [128, 512], BF16))
            sqb = p1ctx.enter_context(nc.sbuf_tensor("sqb", [128, 512], BF16))
            cosb = [p1ctx.enter_context(
                nc.sbuf_tensor(f"cosb{i}", [128, 2048], BF16))
                for i in range(2)]
            cx = p1ctx.enter_context(nc.sbuf_tensor("cx", [128, D], F32))
            shb = [p1ctx.enter_context(
                nc.sbuf_tensor(f"shb{i}", [128, D], BF16)) for i in range(4)]
            acb = [p1ctx.enter_context(
                nc.sbuf_tensor(f"acb{i}", [128, D], F32)) for i in range(2)]
            cps4 = [p1ctx.enter_context(
                nc.psum_tensor(f"cps{i}", [128, 512], F32)) for i in range(4)]
            breB = p1ctx.enter_context(nc.psum_tensor("breB", [128, 512], F32))
            bimB = p1ctx.enter_context(nc.psum_tensor("bimB", [128, 512], F32))
            ptrP = [p1ctx.enter_context(
                nc.psum_tensor(f"ptrP{i}", [128, 1024], BF16))
                for i in range(2)]

            df_sems = ["df_a", "df_b"]
            cs_sems = ["cs_a", "cs_b"]
            gs_sems = ["gs_a", "gs_b", "gs_c", "gs_d"]
            sl2_sems = ["sl2_a", "sl2_a"]
            gtile = 0
            ev_p1 = {}
            xb_credit = [None]
            v_acc7 = {}
            a_ln2_S = {}

            def emit_transposes(src_bf, dst_fn, to_dram):
                nonlocal gtile
                for g in range(4):
                    par = gtile % 2
                    if gtile >= 2:
                        B.wait("tensor", "v", ev_p1[gtile - 2])
                    for j in range(8):
                        B.emit("tensor", lambda e, g=g, j=j, par=par:
                               e.transpose(
                                   out=ptrP[par][:, j * 128:(j + 1) * 128],
                                   in_=src_bf[:, (g * 8 + j) * 128:
                                              (g * 8 + j + 1) * 128],
                                   identity=identb[:]),
                               inc=("t", 1) if j == 7 else None)
                    t_tot = B.cnt["t"]
                    B.wait("vector", "t", t_tot)
                    if to_dram:
                        B.wait("vector", "sl2_a", B.cnt["sl2_a"])
                        B.emit("vector", lambda e, par=par:
                               e.tensor_copy(out=l2st0[:],
                                             in_=ptrP[par][:]),
                               inc=("v", 1))
                        B.drain("vector")
                        ev_p1[gtile] = B.cnt["v"]
                        dst = dst_fn(g)
                        B.wait("sync", "v", ev_p1[gtile])
                        B.emit("sync", lambda e, dst=dst:
                               e.dma_start(
                                   out=dst, in_=l2st0[:].rearrange(
                                       "p (kk h) -> p kk h", h=128)),
                               inc=("sl2_a", 16))
                    else:
                        dst = dst_fn(g)
                        B.emit("vector", lambda e, par=par, dst=dst:
                               e.tensor_copy(
                                   out=dst, in_=ptrP[par][:].rearrange(
                                       "p (kk h) -> p kk h", h=128)),
                               inc=("v", 1))
                        B.drain("vector")
                        ev_p1[gtile] = B.cnt["v"]
                    gtile += 1

            def _p1_tile(tt):
                for s in range(4):
                    S = tt * 4 + s
                    if xb_credit[0] is not None:
                        B.wait("sync", *xb_credit[0])
                    B.emit("sync", lambda e, tt=tt, s=s:
                           e.dma_start(
                               out=xb[:],
                               in_=x_d[tt * 512 + s * 128:
                                       tt * 512 + (s + 1) * 128, :]),
                           inc=("xi_a", 16))
                    xi_tot = B.cnt["xi_a"]
                    B.wait("vector", "xi_a", xi_tot)
                    B.wait("scalar", "xi_a", xi_tot)
                    B.wait("scalar", "t")
                    B.wait("scalar", "gv", B.cnt["gv"])
                    a_norm = _ln_emit(B, xb[:], st_t[:], [castC[s % 2][:]])
                    xb_credit[0] = ("a", a_norm)
                    B.wait("gpsimd", "a", a_norm)
                    for nm in gs_sems:
                        B.wait("gpsimd", nm, B.cnt[nm])
                    B.emit("gpsimd", lambda e, s=s: e.dma_start(
                        out=vv_d[s][:, 0:D], in_=castC[s % 2][:]),
                        inc=("gv", 16))
                    B.emit("gpsimd", lambda e, s=s: e.dma_start(
                        out=vv_d[s][:, D:2 * D], in_=castC[s % 2][:]),
                        inc=("gv", 16))
                    if debug and tt == 0 and s == 0:
                        B.wait("sync", "a", a_norm)
                        B.emit("sync", lambda e, s=s: e.dma_start(
                            out=dbg["ln1"], in_=castC[s % 2][:]),
                            inc=("dbgs", 16))
                    B.wait("tensor", "a", a_norm)

                    def _dst_ln1(g, s=s):
                        return ln1fm[:].rearrange(
                            "p (kc t) -> p kc t", t=512)[
                            :, g * 8:(g + 1) * 8, s * 128:(s + 1) * 128]
                    emit_transposes(castC[s % 2], _dst_ln1, False)
                v_ln1fm_done = B.cnt["v"]

                B.wait("tensor", "v", v_ln1fm_done)
                B.wait("tensor", "a")
                B.wait("scalar", "v")
                B.wait("scalar", "sx1")
                v_p_done = {}
                t_dft_fc = {}
                a_sq_fc = {}
                for fc in range(NFC):
                    par = fc % 2
                    if fc >= 2:
                        B.wait("scalar", "t", t_dft_fc[fc - 2])
                    B.emit("scalar", lambda e, fc=fc, par=par: e.dma_start(
                        out=dfr[par][:, 0:NKC * 128], in_=dftre_d[fc]),
                        inc=(df_sems[par], 16))
                    B.emit("scalar", lambda e, fc=fc, par=par: e.dma_start(
                        out=dfr[par][:, NKC * 128:2 * NKC * 128],
                        in_=dftim_d[fc]),
                        inc=(df_sems[par], 16))
                    df_tot = B.cnt[df_sems[par]]
                    B.wait("tensor", df_sems[par], df_tot)
                    if fc >= 1:
                        B.wait("tensor", "a", a_sq_fc[fc - 1])
                    for kc in range(NKC):
                        B.emit("tensor", lambda e, kc=kc, par=par: e.matmul(
                            breB[:],
                            lhsT=dfr[par][:, kc * 128:(kc + 1) * 128],
                            rhs=ln1fm[:, kc * 512:(kc + 1) * 512],
                            start=(kc == 0), stop=(kc == NKC - 1)))
                    for kc in range(NKC):
                        B.emit("tensor", lambda e, kc=kc, par=par: e.matmul(
                            bimB[:],
                            lhsT=dfr[par][:, NKC * 128 + kc * 128:
                                          NKC * 128 + (kc + 1) * 128],
                            rhs=ln1fm[:, kc * 512:(kc + 1) * 512],
                            start=(kc == 0), stop=(kc == NKC - 1)),
                            inc=("t", 1) if kc == NKC - 1 else None)
                    t_dft_fc[fc] = B.cnt["t"]
                    B.wait("scalar", "t", t_dft_fc[fc])
                    if fc >= 1:
                        B.wait("scalar", "v", v_p_done[fc - 1])
                    B.drain("scalar")
                    B.emit("scalar", lambda e: e.activation(
                        out=sqa[:], in_=breB[:], func=AF.Square))
                    B.drain("scalar")
                    B.emit("scalar", lambda e: e.activation(
                        out=sqb[:], in_=bimB[:], func=AF.Square),
                        inc=("a", 1))
                    B.drain("scalar")
                    a_sq_fc[fc] = B.cnt["a"]
                    B.wait("vector", "a", a_sq_fc[fc])
                    B.emit("vector", lambda e, fc=fc: e.tensor_add(
                        pfm[:, fc * 512:(fc + 1) * 512], sqa[:], sqb[:]),
                        inc=("v", 1))
                    B.drain("vector")
                    v_p_done[fc] = B.cnt["v"]

                def _part_a(s, cgate):
                    S = tt * 4 + s
                    ss = s % 2
                    v_evac = None
                    for half in range(2):
                        ntc = 4 if half == 0 else 1
                        cs_t = {}
                        for fc in range(NFC):
                            par = fc % 2
                            if fc >= 2:
                                B.wait("scalar", "t", cs_t[fc - 2])
                            else:
                                B.wait("scalar", "t")
                            B.emit("scalar", lambda e, fc=fc, half=half,
                                   par=par, ntc=ntc: e.dma_start(
                                       out=cosb[par][:, 0:ntc * 512],
                                       in_=cosm_d[fc][:, half * 2048:
                                                      half * 2048 +
                                                      ntc * 512]),
                                   inc=(cs_sems[par], 16))
                            cs_tot = B.cnt[cs_sems[par]]
                            B.wait("tensor", cs_sems[par], cs_tot)
                            if fc == 0:
                                if half == 0 and cgate is None:
                                    B.wait("tensor", "a")
                                    B.wait("tensor", "v")
                                elif half == 0:
                                    B.wait("tensor", "v",
                                           max(cgate,
                                               v_p_done[NFC - 1]))
                                else:
                                    B.wait("tensor", "v", v_evac)
                            for tc in range(ntc):
                                B.emit("tensor", lambda e, fc=fc, s=s,
                                       tc=tc, par=par: e.matmul(
                                           cps4[tc][:],
                                           lhsT=pfm[:, fc * 512 + s * 128:
                                                    fc * 512 +
                                                    (s + 1) * 128],
                                           rhs=cosb[par][:, tc * 512:
                                                         (tc + 1) * 512],
                                           start=(fc == 0),
                                           stop=(fc == NFC - 1)),
                                       inc=("t", 1) if tc == ntc - 1
                                       else None)
                            cs_t[fc] = B.cnt["t"]
                        t_c = B.cnt["t"]
                        B.wait("vector", "t", t_c)
                        for tc in range(ntc):
                            B.emit("vector", lambda e, tc=tc, half=half:
                                   e.tensor_copy(
                                       out=cx[:, half * 2048 + tc * 512:
                                              half * 2048 + (tc + 1) * 512],
                                       in_=cps4[tc][:]),
                                   inc=("v", 1) if tc == ntc - 1 else None)
                            B.drain("vector")
                        v_evac = B.cnt["v"]
                    B.emit("vector", lambda e: e.tensor_copy(
                        out=cx[:, 2560:4096],
                        in_=bass.AP(cx, 1536, [[4096, 128], [-1, 1536]])),
                        inc=("v", 1))
                    B.drain("vector")
                    v_c = B.cnt["v"]
                    if debug and tt == 0 and s == 0:
                        B.wait("sync", "v", v_c)
                        B.emit("sync", lambda e: e.dma_start(out=dbg["c"],
                                                             in_=cx[:]),
                               inc=("dbgs", 16))
                    B.emit("vector", lambda e, ss=ss: e.max(
                        out=top8[ss][:], in_=cx[:]))
                    B.drain("vector")
                    B.emit("vector", lambda e, ss=ss: e.max_index(
                        out=idx8u[ss][:], in_max=top8[ss][:],
                        in_values=cx[:]))
                    B.drain("vector")
                    B.emit("vector", lambda e, ss=ss: e.tensor_scalar(
                        out=z8[ss][:], in0=top8[ss][:],
                        scalar1=top8[ss][:, 0:1],
                        scalar2=-80.0, op0=ALU.subtract, op1=ALU.max),
                        inc=("v", 1))
                    B.drain("vector")
                    v_z = B.cnt["v"]
                    B.wait("scalar", "v", v_z)
                    B.drain("scalar")
                    B.emit("scalar", lambda e, ss=ss: e.activation(
                        out=e8[ss][:], in_=z8[ss][:], func=AF.Exp),
                        inc=("a", 1))
                    B.drain("scalar")
                    a_e = B.cnt["a"]
                    B.wait("vector", "a", a_e)
                    B.emit("vector", lambda e, ss=ss: e.reduce_sum(
                        out=st_t[:, 14:15], in_=e8[ss][:], axis=AX.X))
                    B.drain("vector")
                    B.emit("vector", lambda e: e.reciprocal(
                        out=st_t[:, 15:16], in_=st_t[:, 14:15]))
                    B.drain("vector")
                    B.emit("vector", lambda e, ss=ss: e.tensor_scalar(
                        out=w8[ss][:], in0=e8[ss][:],
                        scalar1=st_t[:, 15:16],
                        scalar2=None, op0=ALU.mult))
                    B.drain("vector")
                    B.emit("vector", lambda e, ss=ss: e.tensor_copy(
                        out=d32[ss][:], in_=idx8u[ss][:]))
                    B.drain("vector")
                    B.emit("vector", lambda e, ss=ss: e.tensor_add(
                        fidx[ss][:], d32[ss][:],
                        rowb[:].broadcast_to([128, 8])))
                    B.drain("vector")
                    B.emit("vector", lambda e, ss=ss: e.tensor_copy(
                        out=gidxu[ss][:], in_=fidx[ss][:]),
                        inc=("v", 1))
                    B.drain("vector")
                    v_idx = B.cnt["v"]
                    if debug and tt == 0 and s == 0:
                        B.wait("sync", "v", v_idx)
                        B.emit("sync", lambda e, ss=ss: e.dma_start(
                            out=dbg["top8"], in_=top8[ss][:]),
                            inc=("dbgs", 16))
                        B.emit("sync", lambda e, ss=ss: e.dma_start(
                            out=dbg["idx8"], in_=idx8u[ss][:]),
                            inc=("dbgs", 16))
                    vv_flat = vv_d[s].rearrange("p f -> (p f)").unsqueeze(1)
                    B.wait("gpsimd", "v", v_idx)
                    B.wait("gpsimd", "gv", B.cnt["gv"])
                    if S >= 2:
                        B.wait("gpsimd", "v", v_acc7[S - 2])
                    gs_tots = []
                    for k in range(2):
                        gp = 2 * ss + (k % 2)
                        B.emit("gpsimd",
                               lambda e, k=k, gp=gp, ss=ss,
                               vv_flat=vv_flat:
                               e.indirect_dma_start(
                                   out=shb[gp][:], out_offset=None,
                                   in_=vv_flat,
                                   in_offset=IndirectOffsetOnAxis(
                                       gidxu[ss][:, k:k + 1], 0)),
                               inc=(gs_sems[gp], 16))
                        gs_tots.append(B.cnt[gs_sems[gp]])
                    return v_c, (gs_tots, vv_flat)

                def _part_b(s, ga):
                    gs_tots, vv_flat = ga
                    S = tt * 4 + s
                    ss = s % 2
                    if S >= 1 and (S - 1) in a_ln2_S:
                        B.wait("vector", "a", a_ln2_S[S - 1])
                    B.wait("vector", "sx1")
                    v_acc = {}
                    for k in range(8):
                        gp = 2 * ss + (k % 2)
                        if k >= 2:
                            B.wait("gpsimd", "v", v_acc[k - 2])
                            B.emit("gpsimd",
                                   lambda e, k=k, gp=gp, ss=ss,
                                   vv_flat=vv_flat:
                                   e.indirect_dma_start(
                                       out=shb[gp][:], out_offset=None,
                                       in_=vv_flat,
                                       in_offset=IndirectOffsetOnAxis(
                                           gidxu[ss][:, k:k + 1], 0)),
                                   inc=(gs_sems[gp], 16))
                            gs_tots.append(B.cnt[gs_sems[gp]])
                        B.wait("vector", gs_sems[gp], gs_tots[k])
                        if k == 0:
                            B.emit("vector", lambda e, gp=gp, ss=ss:
                                   e.tensor_scalar(
                                       out=acb[0][:], in0=shb[gp][:],
                                       scalar1=w8[ss][:, 0:1], scalar2=None,
                                       op0=ALU.mult), inc=("v", 1))
                        else:
                            B.emit("vector", lambda e, k=k, gp=gp, ss=ss:
                                   e.scalar_tensor_tensor(
                                       out=acb[k % 2][:], in0=shb[gp][:],
                                       scalar=w8[ss][:, k:k + 1],
                                       in1=acb[(k - 1) % 2][:],
                                       op0=ALU.mult, op1=ALU.add),
                                   inc=("v", 1))
                        B.drain("vector")
                        v_acc[k] = B.cnt["v"]
                    v_acc7[S] = v_acc[7]
                    attn_ap = acb[1][:]
                    if debug and tt == 0 and s == 0:
                        B.wait("sync", "v", v_acc[7])
                        B.emit("sync", lambda e: e.dma_start(
                            out=dbg["attn"], in_=attn_ap), inc=("dbgs", 16))
                    B.wait("vector", "v", v_acc[7])
                    B.wait("scalar", "v", v_acc[7])
                    _ln_emit(B, attn_ap, st_t[:], [acb[0][:]])
                    a_lnat = B.cnt["a"]
                    B.wait("sync", "a", a_lnat)
                    B.emit("sync", lambda e, tt=tt, s=s: e.dma_start(
                        out=acb[1][:],
                        in_=x_d[tt * 512 + s * 128:
                                tt * 512 + (s + 1) * 128, :]),
                        inc=("xi_b", 16))
                    xr_tot = B.cnt["xi_b"]
                    B.wait("vector", "a", a_lnat)
                    B.wait("vector", "xi_b", xr_tot)
                    B.emit("vector", lambda e: e.tensor_add(
                        acb[0][:], acb[0][:], acb[1][:]), inc=("v", 1))
                    B.drain("vector")
                    v_x1 = B.cnt["v"]
                    if debug and tt == 0 and s == 0:
                        B.wait("sync", "v", v_x1)
                        B.emit("sync", lambda e: e.dma_start(
                            out=dbg["x1"], in_=acb[0][:]), inc=("dbgs", 16))
                    B.wait("sync", "v", v_x1)
                    B.emit("sync", lambda e, tt=tt, s=s: e.dma_start(
                        out=x1_d[tt * 512 + s * 128:
                                 tt * 512 + (s + 1) * 128, :],
                        in_=acb[0][:]), inc=("sx1", 16))
                    B.wait("scalar", "v", v_x1)
                    B.wait("scalar", "t")
                    B.wait("scalar", "gv", B.cnt["gv"])
                    _ln_emit(B, acb[0][:], st_t[:], [castC[s % 2][:]])
                    a_ln2_S[S] = B.cnt["a"]
                    return a_ln2_S[S]

                def _tr(s, a_ln2):
                    B.wait("tensor", "a", a_ln2)

                    def _dst_ln2(g, tt=tt, s=s):
                        return ln2_d.rearrange(
                            "(kc p) t -> p kc t", p=128)[
                            :, g * 8:(g + 1) * 8,
                            tt * 512 + s * 128: tt * 512 + (s + 1) * 128]
                    emit_transposes(castC[s % 2], _dst_ln2, True)

                return _part_a, _part_b, _tr

            _cg = None
            for _tt in range(NTILE):
                _pa, _pb, _ptr = _p1_tile(_tt)
                _ga = {}
                _cg, _ga[0] = _pa(0, _cg)
                _cg, _ga[1] = _pa(1, _cg)
                _a2 = _pb(0, _ga[0])
                _cg, _ga[2] = _pa(2, _cg)
                _a2b = _pb(1, _ga[1])
                _ptr(0, _a2)
                _cg, _ga[3] = _pa(3, _cg)
                _a2c = _pb(2, _ga[2])
                _ptr(1, _a2b)
                _a2d = _pb(3, _ga[3])
                _ptr(2, _a2c)
                _ptr(3, _a2d)

            p1_sx1 = B.cnt["sx1"]
            p1_sl2a = B.cnt["sl2_a"]
            p1_sl2b = B.cnt["sl2_b"]

    for eng in ENGINES:
        B.wait(eng, "sx1", B.cnt["sx1"])
        B.wait(eng, "sl2_a", B.cnt["sl2_a"])
        B.wait(eng, "sl2_b", B.cnt["sl2_b"])
        B.wait(eng, "gv")
        B.wait(eng, "gx")
        B.wait(eng, "gs_a")
        B.wait(eng, "gs_b")
        B.wait(eng, "gs_c")
        B.wait(eng, "gs_d")
        if debug:
            B.wait(eng, "dbgs")

    if "2" in phases:
        for eng in ENGINES:
            B.wait(eng, "sx1", B.cnt["sx1"])
            B.wait(eng, "sl2_a", B.cnt["sl2_a"])
            B.wait(eng, "sl2_b", B.cnt["sl2_b"])
            B.wait(eng, "gv")
            B.wait(eng, "gs_a")
            B.wait(eng, "gs_b")
            if debug:
                B.wait(eng, "dbgs")

        with ExitStack() as p2ctx:
            actb = p2ctx.enter_context(
                nc.sbuf_tensor("actb", [128, NHC * 512], BF16))
            st2 = p2ctx.enter_context(nc.sbuf_tensor("st2", [128, 16], F32))
            ytsb = p2ctx.enter_context(nc.sbuf_tensor("ytsb", [128, D], F32))
            outb = [p2ctx.enter_context(nc.sbuf_tensor(
                f"outb{i}", [128, D], F32)) for i in range(2)]
            pg = [p2ctx.enter_context(
                nc.psum_tensor(f"pg{i}", [128, 512], F32)) for i in range(2)]
            pu = [p2ctx.enter_context(
                nc.psum_tensor(f"pu{i}", [128, 512], F32)) for i in range(2)]
            pyf = [p2ctx.enter_context(
                nc.psum_tensor(f"pyf{i}", [128, 512], F32)) for i in range(2)]
            pyt = [p2ctx.enter_context(
                nc.psum_tensor(f"pyt{i}", [128, 512], F32)) for i in range(2)]

            w12_sems = ["w12_a", "w12_b"]
            d3_sems = ["d3_a", "d3_b"]
            gy_sems = ["gy_a", "gy_b"]
            so_sems = ["so_a", "so_b"]

            B.wait("sync", "dqs_a", p0_dqs_a)
            B.wait("sync", "dqs_b", p0_dqs_b)
            B.wait("sync", "sl2_a", B.cnt["sl2_a"])
            B.wait("sync", "sl2_b", B.cnt["sl2_b"])
            B.wait("sync", "sx1", B.cnt["sx1"])
            B.wait("sync", "cc", p0_cc)
            B.wait("scalar", "cc", p0_cc)
            B.wait("scalar", "dqs_a", p0_dqs_a)
            B.wait("scalar", "dqs_b", p0_dqs_b)

            def _p2_tile(tt):
                with ExitStack() as m12ctx:
                    l2s = m12ctx.enter_context(nc.sbuf_tensor(
                        f"l2s{tt}", [128, NKC * 512], BF16))
                    w1s = [m12ctx.enter_context(nc.sbuf_tensor(
                        f"w1s{tt}_{i}", [128, NKC * 128], BF16))
                        for i in range(2)]
                    w2s = [m12ctx.enter_context(nc.sbuf_tensor(
                        f"w2s{tt}_{i}", [128, NKC * 128], BF16))
                        for i in range(2)]
                    sgs = [m12ctx.enter_context(nc.sbuf_tensor(
                        f"sgs{tt}_{i}", [128, 512], F32)) for i in range(2)]

                    l2dst = l2s[:].rearrange("p (kc t) -> p kc t", t=512)
                    l2src = ln2_d.rearrange("(kc p) t -> p kc t", p=128)[
                        :, :, tt * 512:(tt + 1) * 512]
                    B.wait("sync", "t")
                    B.wait("sync", "gy_a", B.cnt["gy_a"])
                    B.wait("sync", "gy_b", B.cnt["gy_b"])
                    B.wait("vector", "a")
                    B.emit("sync", lambda e, l2dst=l2dst, l2src=l2src:
                           e.dma_start(out=l2dst, in_=l2src), inc=("l2l", 16))
                    l2_tot = B.cnt["l2l"]
                    B.wait("tensor", "l2l", l2_tot)
                    B.wait("tensor", "v")

                    t_mm = {}
                    v_act = {}
                    w1_sems = ["w12_a", "w12_b"]
                    w2_sems = ["w2_a", "w2_b"]
                    for hc in range(NHC):
                        par = hc % 2
                        sl3 = hc % 2
                        if hc >= 2:
                            B.wait("sync", "t", t_mm[hc - 2])
                        B.emit("sync", lambda e, hc=hc, sl3=sl3: e.dma_start(
                            out=w1s[sl3][:], in_=dq1_d[hc]),
                            inc=(w1_sems[sl3], 16))
                        B.emit("sync", lambda e, hc=hc, par=par:
                               e.dma_start(
                                   out=w2s[par][:], in_=dq2_d[hc]),
                               inc=(w2_sems[par], 16))
                        w1_tot = B.cnt[w1_sems[sl3]]
                        w2_tot = B.cnt[w2_sems[par]]
                        B.wait("tensor", w1_sems[sl3], w1_tot)
                        B.wait("tensor", w2_sems[par], w2_tot)
                        if hc >= 2:
                            B.wait("tensor", "v", v_act[hc - 2])
                        for kc in range(NKC):
                            B.emit("tensor", lambda e, kc=kc, par=par,
                                   sl3=sl3: e.matmul(
                                pg[par][:],
                                lhsT=w1s[sl3][:, kc * 128:(kc + 1) * 128],
                                rhs=l2s[:, kc * 512:(kc + 1) * 512],
                                start=(kc == 0), stop=(kc == NKC - 1)))
                        for kc in range(NKC):
                            B.emit("tensor", lambda e, kc=kc, par=par,
                                   sl3=sl3: e.matmul(
                                pu[par][:],
                                lhsT=w2s[par][:, kc * 128:(kc + 1) * 128],
                                rhs=l2s[:, kc * 512:(kc + 1) * 512],
                                start=(kc == 0), stop=(kc == NKC - 1)),
                                inc=("t", 1) if kc == NKC - 1 else None)
                        t_mm[hc] = B.cnt["t"]
                        B.wait("scalar", "t", t_mm[hc])
                        if hc >= 2:
                            B.wait("scalar", "v", v_act[hc - 2])
                        B.drain("scalar")
                        B.emit("scalar", lambda e, par=par: e.activation(
                            out=sgs[par][:], in_=pg[par][:], func=AF.Silu),
                            inc=("a", 1))
                        B.drain("scalar")
                        a_si = B.cnt["a"]
                        B.wait("vector", "a", a_si)
                        B.emit("vector", lambda e, hc=hc, par=par:
                               e.tensor_tensor(
                                   out=actb[:, hc * 512:(hc + 1) * 512],
                                   in0=sgs[par][:], in1=pu[par][:],
                                   op=ALU.mult), inc=("v", 1))
                        B.drain("vector")
                        v_act[hc] = B.cnt["v"]
                    v_act_done = B.cnt["v"]
                    B.wait("tensor", "v", v_act_done)

                with ExitStack() as m3ctx:
                    d3s = [m3ctx.enter_context(nc.sbuf_tensor(
                        f"d3s{tt}_{i}", [128, NHC * 128], BF16))
                        for i in range(2)]
                    yst = [m3ctx.enter_context(nc.sbuf_tensor(
                        f"yst{tt}_{i}", [128, 512], F32)) for i in range(2)]
                    ytst = [m3ctx.enter_context(nc.sbuf_tensor(
                        f"ytst{tt}_{i}", [128, 512], F32)) for i in range(2)]

                    t_m3 = {}
                    v_yev = {}
                    t_ytr = {}
                    v_ytev = {}
                    for dc in range(NKC):
                        par = dc % 2
                        if dc >= 2:
                            B.wait("scalar", "t", t_m3[dc - 2])
                        else:
                            B.wait("scalar", "v", v_act_done)
                        B.emit("scalar", lambda e, dc=dc, par=par: e.dma_start(
                            out=d3s[par][:], in_=dq3_d[dc]),
                            inc=(d3_sems[par], 16))
                        d3_tot = B.cnt[d3_sems[par]]
                        B.wait("tensor", d3_sems[par], d3_tot)
                        if dc >= 2:
                            B.wait("tensor", "v", v_yev[dc - 2])
                        for hc in range(NHC):
                            B.emit("tensor", lambda e, hc=hc, par=par: e.matmul(
                                pyf[par][:],
                                lhsT=d3s[par][:, hc * 128:(hc + 1) * 128],
                                rhs=actb[:, hc * 512:(hc + 1) * 512],
                                start=(hc == 0), stop=(hc == NHC - 1)),
                                inc=("t", 1) if hc == NHC - 1 else None)
                        t_m3[dc] = B.cnt["t"]
                        B.wait("vector", "t", t_m3[dc])
                        B.emit("vector", lambda e, par=par: e.tensor_copy(
                            out=yst[par][:], in_=pyf[par][:]), inc=("v", 1))
                        B.drain("vector")
                        v_yev[dc] = B.cnt["v"]
                        B.wait("tensor", "v", v_yev[dc])
                        if dc >= 2:
                            B.wait("tensor", "v", v_ytev[dc - 2])
                        for sj in range(4):
                            B.emit("tensor", lambda e, sj=sj, par=par: e.matmul(
                                pyt[par][:, sj * 128:(sj + 1) * 128],
                                lhsT=yst[par][:, sj * 128:(sj + 1) * 128],
                                rhs=identf[:], start=True, stop=True),
                                inc=("t", 1) if sj == 3 else None)
                        t_ytr[dc] = B.cnt["t"]
                        B.wait("vector", "t", t_ytr[dc])
                        B.wait("vector", gy_sems[par], B.cnt[gy_sems[par]])
                        B.emit("vector", lambda e, par=par: e.tensor_copy(
                            out=ytst[par][:], in_=pyt[par][:]), inc=("v", 1))
                        B.drain("vector")
                        v_ytev[dc] = B.cnt["v"]
                        ydst = ytm_d.rearrange("s p d -> p s d")[
                            :, :, dc * 128:(dc + 1) * 128]
                        B.wait("gpsimd", "v", v_ytev[dc])
                        B.emit("gpsimd", lambda e, ydst=ydst, par=par:
                               e.dma_start(
                                   out=ydst, in_=ytst[par][:].rearrange(
                                       "p (s d) -> p s d", d=128)),
                               inc=(gy_sems[par], 16))
                    gy_a_tot = B.cnt["gy_a"]
                    gy_b_tot = B.cnt["gy_b"]

                B.wait("sync", "gy_a", gy_a_tot)
                B.wait("sync", "gy_b", gy_b_tot)
                x1_sems = ["xi_a", "xi_b"]
                for s in range(4):
                    par = s % 2
                    B.wait("sync", "a")
                    B.emit("sync", lambda e, s=s: e.dma_start(
                        out=ytsb[:], in_=ytm_d[s]), inc=("ye_a", 16))
                    ye0 = B.cnt["ye_a"]
                    B.wait("sync", so_sems[par], B.cnt[so_sems[par]])
                    B.emit("sync", lambda e, tt=tt, s=s, par=par:
                           e.dma_start(
                               out=outb[par][:],
                               in_=x1_d[tt * 512 + s * 128:
                                        tt * 512 + (s + 1) * 128, :]),
                           inc=(x1_sems[par], 16))
                    ye1 = B.cnt[x1_sems[par]]
                    B.wait("vector", "ye_a", ye0)
                    B.wait("scalar", "ye_a", ye0)
                    _ln_emit(B, ytsb[:], st2[:], [ytsb[:]],
                             sqscr=actb[:, 0:D])
                    a_ln3 = B.cnt["a"]
                    B.wait("vector", "a", a_ln3)
                    B.wait("vector", x1_sems[par], ye1)
                    B.emit("vector", lambda e, par=par: e.tensor_add(
                        outb[par][:], outb[par][:], ytsb[:]), inc=("v", 1))
                    B.drain("vector")
                    v_out = B.cnt["v"]
                    if debug and tt == 0 and s == 0:
                        B.wait("sync", "v", v_out)
                        B.emit("sync", lambda e: e.dma_start(
                            out=dbg["yt"], in_=ytsb[:]), inc=("dbgs", 16))
                    B.wait("sync", "v", v_out)
                    B.emit("sync", lambda e, tt=tt, s=s, par=par:
                           e.dma_start(
                               out=out_d[tt * 512 + s * 128:
                                         tt * 512 + (s + 1) * 128, :],
                               in_=outb[par][:]), inc=(so_sems[par], 16))

            for _tt in range(NTILE):
                _p2_tile(_tt)

            B.wait("sync", "so_a", B.cnt["so_a"])
            B.wait("sync", "so_b", B.cnt["so_b"])
            if debug:
                B.wait("sync", "dbgs", B.cnt["dbgs"])

    for nm in ("dqs_a", "dqs_b", "sx1", "sl2_a", "sl2_b", "gv",
               "gs_a", "gs_b", "gs_c", "gs_d", "gy_a", "gy_b", "so_a",
               "so_b", "cc", "gx"):
        B.wait("sync", nm)
    B.finalize_block()
    B.ctx.close()
    return nc


_CACHE = {}


def kernel(L0_attn_x, L0_ffn_w1, L0_ffn_w2, L0_ffn_w3):
    x = np.ascontiguousarray(np.asarray(L0_attn_x, dtype=np.float32)
                             .reshape(16384, D))
    w1 = np.ascontiguousarray(np.asarray(L0_ffn_w1, dtype=np.float32))
    w2 = np.ascontiguousarray(np.asarray(L0_ffn_w2, dtype=np.float32))
    w3 = np.ascontiguousarray(np.asarray(L0_ffn_w3, dtype=np.float32))
    if "consts" not in _CACHE:
        _CACHE["consts"] = _build_consts()
    consts = _CACHE["consts"]
    T = 16384 // NCORES
    nc = build_nc(T=T)
    w1p = np.concatenate([w1, np.zeros((88 * 128 - H, D), np.float32)], 0)
    w2p = np.concatenate([w2, np.zeros((88 * 128 - H, D), np.float32)], 0)
    r1 = (88 // NCORES) * 128
    r3 = (NKC // NCORES) * 128
    in_maps = []
    for i in range(NCORES):
        m = {"x": x[i * T:(i + 1) * T],
             "w1s": np.ascontiguousarray(w1p[i * r1:(i + 1) * r1]),
             "w2s": np.ascontiguousarray(w2p[i * r1:(i + 1) * r1]),
             "w3s": np.ascontiguousarray(w3[i * r3:(i + 1) * r3])}
        m.update(consts)
        in_maps.append(m)
    res = run_bass_kernel_spmd(nc, in_maps, core_ids=list(range(NCORES)))
    out = np.concatenate([res.results[i]["out"] for i in range(NCORES)],
                         axis=0)
    return out.reshape(2, 8192, D)


# version 12
# speedup vs baseline: 1.3947x; 1.3947x over previous
from contextlib import ExitStack

import numpy as np
import ml_dtypes

import concourse.bass as bass
import concourse.mybir as mybir
from concourse.bass import IndirectOffsetOnAxis
from concourse.bass_utils import run_bass_kernel_spmd

F32 = mybir.dt.float32
BF16 = mybir.dt.bfloat16
U32 = mybir.dt.uint32
AF = mybir.ActivationFunctionType
ALU = mybir.AluOpType
AX = mybir.AxisListType

NCORES = 8
D = 4096
H = 11008
NHC = H // 128
NKC = D // 128
FQ = 2176
NFC = FQ // 128
EPS = 1e-5

NF4_CODE = np.array(
    [-1.0, -0.6961928009986877, -0.5250730514526367, -0.39491748809814453,
     -0.28444138169288635, -0.18477343022823334, -0.09105003625154495, 0.0,
     0.07958029955625534, 0.16093020141124725, 0.24611230194568634,
     0.33791524171829224, 0.44070982933044434, 0.5626170039176941,
     0.7229568362236023, 1.0], dtype=np.float32)
NF4_BOUNDS = ((NF4_CODE[:-1] + NF4_CODE[1:]) * np.float32(0.5)).astype(np.float32)
NF4_DELTA = (NF4_CODE[1:] - NF4_CODE[:-1]).astype(np.float32)

ENGINES = ("sync", "vector", "scalar", "tensor", "gpsimd")


def _build_consts():
    out = {}
    dd = np.arange(D, dtype=np.int64)
    ff = np.arange(FQ, dtype=np.int64)
    dmat = dd[:, None] * ff[None, :]
    ang = 2.0 * np.pi * ((dmat % D).astype(np.float64)) / D
    re = np.cos(ang)
    im = np.sin(ang)
    re[:, 2049:] = 0.0
    im[:, 2049:] = 0.0

    def _lhsT_tiles(m):
        m4 = m.reshape(NKC, 128, NFC, 128)
        return np.ascontiguousarray(
            m4.transpose(2, 1, 0, 3).reshape(NFC, 128, NKC * 128))

    out["dftre"] = _lhsT_tiles(re).astype(ml_dtypes.bfloat16)
    out["dftim"] = _lhsT_tiles(im).astype(ml_dtypes.bfloat16)
    w = np.zeros(FQ, dtype=np.float64)
    w[0] = 1.0 / D
    w[1:2048] = 2.0 / D
    w[2048] = 1.0 / D
    tt_ = np.arange(D, dtype=np.int64)
    fm = ff[:, None] * tt_[None, :]
    cosm = w[:, None] * np.cos(2.0 * np.pi * ((fm % D).astype(np.float64)) / D)
    cosm[2049:, :] = 0.0
    out["cosm"] = np.ascontiguousarray(
        cosm.reshape(NFC, 128, D)).astype(ml_dtypes.bfloat16)
    out["identb"] = np.eye(128, dtype=ml_dtypes.bfloat16)
    out["identf"] = np.eye(128, dtype=np.float32)
    out["rowbase"] = (np.arange(128, dtype=np.float32) * 8192.0).reshape(128, 1)
    return out


class KB:

    def __init__(self, nc):
        self.nc = nc
        self.steps = {e: [] for e in ENGINES}
        self.sem = {}
        self.cnt = {}
        self.ctx = ExitStack()

    def new_sem(self, *names):
        for n in names:
            self.sem[n] = self.ctx.enter_context(self.nc.semaphore(n))
            self.cnt[n] = 0

    def wait(self, eng, name, val=None):
        v = self.cnt[name] if val is None else val
        h = self.sem[name]
        if v > 0:
            self.steps[eng].append(lambda e, h=h, v=v: e.wait_ge(h, v))
        return v

    def emit(self, eng, fn, inc=None):
        pair = None
        if inc is not None:
            nm, n = inc
            self.cnt[nm] += n
            pair = (self.sem[nm], n)

        def step(e, fn=fn, pair=pair):
            ins = fn(e)
            if pair is not None:
                ins.then_inc(pair[0], pair[1])
        self.steps[eng].append(step)
        return self.cnt[inc[0]] if inc else None

    def drain(self, eng):
        self.steps[eng].append(lambda e: e.drain())

    def finalize_block(self):
        block = self.ctx.enter_context(self.nc.Block())
        for eng in ENGINES:
            steps = self.steps[eng]
            if not steps:
                continue
            dec = getattr(block, eng)

            def section(e, steps=steps):
                for s in steps:
                    s(e)
            dec(section)


def _ln_emit(B, src_ap, st, outs, sqscr=None):
    sm = st[:, 0:1]; sq = st[:, 1:2]; m = st[:, 2:3]; msq = st[:, 3:4]
    m2 = st[:, 4:5]; var = st[:, 5:6]; vpe = st[:, 6:7]; sd = st[:, 7:8]
    r0 = st[:, 8:9]; r0s = st[:, 9:10]; tm = st[:, 10:11]; co = st[:, 11:12]
    r = st[:, 12:13]; nmr = st[:, 13:14]

    B.wait("scalar", "v")
    B.drain("scalar")
    B.emit("vector", lambda e: e.reduce_sum(out=sm, in_=src_ap, axis=AX.X),
           inc=("v", 1))
    B.drain("vector")
    scr = outs[0] if sqscr is None else sqscr
    B.emit("scalar", lambda e: e.activation(out=scr, in_=src_ap,
                                            func=AF.Square, accum_out=sq),
           inc=("a", 1))
    B.drain("scalar")
    B.wait("vector", "a")
    B.emit("vector", lambda e: e.tensor_scalar_mul(m, sm, 1.0 / 4096.0))
    B.drain("vector")
    B.emit("vector", lambda e: e.tensor_scalar_mul(msq, sq, 1.0 / 4096.0))
    B.drain("vector")
    B.emit("vector", lambda e: e.tensor_mul(m2, m, m))
    B.drain("vector")
    B.emit("vector", lambda e: e.tensor_sub(var, msq, m2))
    B.drain("vector")
    B.emit("vector", lambda e: e.tensor_scalar_add(vpe, var, float(EPS)),
           inc=("v", 1))
    B.drain("vector")
    B.wait("scalar", "v")
    B.emit("scalar", lambda e: e.activation(out=sd, in_=vpe, func=AF.Sqrt),
           inc=("a", 1))
    B.drain("scalar")
    B.wait("vector", "a")
    B.emit("vector", lambda e: e.reciprocal(out=r0, in_=sd))
    B.drain("vector")
    B.emit("vector", lambda e: e.tensor_mul(r0s, r0, r0))
    B.drain("vector")
    B.emit("vector", lambda e: e.tensor_mul(tm, r0s, vpe))
    B.drain("vector")
    B.emit("vector", lambda e: e.tensor_scalar(out=co, in0=tm, scalar1=-0.5,
                                               scalar2=1.5, op0=ALU.mult,
                                               op1=ALU.add))
    B.drain("vector")
    B.emit("vector", lambda e: e.tensor_mul(r, r0, co))
    B.drain("vector")
    B.emit("vector", lambda e: e.scalar_tensor_tensor(
        out=nmr, in0=m, scalar=-1.0, in1=r, op0=ALU.mult, op1=ALU.mult),
        inc=("v", 1))
    B.drain("vector")
    B.wait("scalar", "v")
    for i, o in enumerate(outs):
        B.emit("scalar", lambda e, o=o: e.activation(
            out=o, in_=src_ap, func=AF.Identity, bias=nmr, scale=r),
            inc=("a", 1) if i == len(outs) - 1 else None)
        B.drain("scalar")
    return B.cnt["a"]


def build_nc(T=2048, debug=False, phases="012", ncores=8):
    nc = bass.Bass("TRN2", target_bir_lowering=False, debug=False,
                   num_devices=ncores)
    NTILE = T // 512

    W1T = 88 // ncores
    W3T = NKC // ncores
    x_d = nc.dram_tensor("x", [T, D], F32, kind="ExternalInput").ap()
    w1_d = nc.dram_tensor("w1s", [W1T * 128, D], F32, kind="ExternalInput").ap()
    w2_d = nc.dram_tensor("w2s", [W1T * 128, D], F32, kind="ExternalInput").ap()
    w3_d = nc.dram_tensor("w3s", [W3T * 128, H], F32, kind="ExternalInput").ap()
    dftre_d = nc.dram_tensor("dftre", [NFC, 128, NKC * 128], BF16,
                             kind="ExternalInput").ap()
    dftim_d = nc.dram_tensor("dftim", [NFC, 128, NKC * 128], BF16,
                             kind="ExternalInput").ap()
    cosm_d = nc.dram_tensor("cosm", [NFC, 128, D], BF16,
                            kind="ExternalInput").ap()
    identb_d = nc.dram_tensor("identb", [128, 128], BF16,
                              kind="ExternalInput").ap()
    identf_d = nc.dram_tensor("identf", [128, 128], F32,
                              kind="ExternalInput").ap()
    rowb_d = nc.dram_tensor("rowbase", [128, 1], F32,
                            kind="ExternalInput").ap()
    out_d = nc.dram_tensor("out", [T, D], F32, kind="ExternalOutput").ap()

    _ccsp = "Shared" if ncores > 4 else "Local"
    c1_d = nc.dram_tensor("c1T", [W1T, 128, NKC * 128], BF16,
                          kind="Internal").ap()
    c2_d = nc.dram_tensor("c2T", [W1T, 128, NKC * 128], BF16,
                          kind="Internal").ap()
    c3_d = nc.dram_tensor("c3T", [W3T, 128, NHC * 128], BF16,
                          kind="Internal").ap()
    dq1_d = nc.dram_tensor("dq1T", [ncores * W1T, 128, NKC * 128], BF16,
                           kind="Internal", addr_space=_ccsp).ap()
    dq2_d = nc.dram_tensor("dq2T", [ncores * W1T, 128, NKC * 128], BF16,
                           kind="Internal", addr_space=_ccsp).ap()
    dq3_d = nc.dram_tensor("dq3T", [ncores * W3T, 128, NHC * 128], BF16,
                           kind="Internal", addr_space=_ccsp).ap()
    x1_d = nc.dram_tensor("x1d", [T, D], F32, kind="Internal").ap()
    ln2_d = nc.dram_tensor("ln2d", [D, T], BF16, kind="Internal").ap()
    vv_d = [nc.dram_tensor(f"vvd{i}", [128, 2 * D], F32,
                           kind="Internal").ap() for i in range(4)]
    ytm_d = nc.dram_tensor("ytmd", [4, 128, D], F32, kind="Internal").ap()

    dbg = {}
    if debug:
        def dbgt(name, shape, dtype=F32):
            dbg[name] = nc.dram_tensor("dbg_" + name, shape, dtype,
                                       kind="ExternalOutput").ap()
        dbgt("ln1", [128, D]); dbgt("c", [128, D]); dbgt("top8", [128, 8])
        dbgt("idx8", [128, 8], U32); dbgt("attn", [128, D])
        dbgt("x1", [128, D]); dbgt("dq1", [128, NKC * 128])
        dbgt("yt", [128, D])

    B = KB(nc)
    B.new_sem("v", "a", "t", "cst",
              "wl_a", "wl_b", "dqs_a", "dqs_b",
              "xi_a", "xi_b", "gv", "gx", "gp", "df_a", "df_b", "cs_a", "cs_b",
              "gs_a", "gs_b", "gs_c", "gs_d", "sx1", "sl2_a", "sl2_b",
              "l2l", "w12_a", "w12_b", "w12_c", "w2_a", "w2_b", "w2_c",
              "d3_a", "d3_b", "gy_a", "gy_b",
              "ye_a", "ye_b", "so_a", "so_b", "cc", "dbgs")
    ctx = B.ctx

    identb = ctx.enter_context(nc.sbuf_tensor("identb_s", [128, 128], BF16))
    identf = ctx.enter_context(nc.sbuf_tensor("identf_s", [128, 128], F32))
    rowb = ctx.enter_context(nc.sbuf_tensor("rowb_s", [128, 1], F32))
    B.emit("sync", lambda e: e.dma_start(out=identb[:], in_=identb_d),
           inc=("cst", 16))
    B.emit("sync", lambda e: e.dma_start(out=identf[:], in_=identf_d),
           inc=("cst", 16))
    B.emit("sync", lambda e: e.dma_start(out=rowb[:], in_=rowb_d),
           inc=("cst", 16))
    const_ld = B.cnt["cst"]
    for eng in ("vector", "scalar", "tensor", "gpsimd"):
        B.wait(eng, "cst", const_ld)

    with ExitStack() as p0ctx:
        FMAX = 5504
        wt = [p0ctx.enter_context(nc.sbuf_tensor(f"wt{i}", [128, FMAX], F32))
              for i in range(2)]
        nb_t = p0ctx.enter_context(nc.sbuf_tensor("nbuf", [128, FMAX], F32))
        acc_t = p0ctx.enter_context(nc.sbuf_tensor("acc", [128, FMAX], F32))
        pl_t = p0ctx.enter_context(nc.sbuf_tensor("plane", [128, FMAX], F32))
        am_t = p0ctx.enter_context(nc.sbuf_tensor("am", [128, 92], F32))
        eq_t = p0ctx.enter_context(nc.sbuf_tensor("eqz", [128, 92], F32))
        ag_t = p0ctx.enter_context(nc.sbuf_tensor("amg", [128, 92], F32))
        dq_t = [p0ctx.enter_context(
            nc.sbuf_tensor(f"dqb{i}", [128, FMAX], BF16)) for i in range(2)]
        accg_t = p0ctx.enter_context(
            nc.sbuf_tensor("accg", [128, FMAX], F32))
        plg_t = p0ctx.enter_context(
            nc.sbuf_tensor("plg", [128, FMAX], F32))
        stg = [p0ctx.enter_context(nc.sbuf_tensor(f"stg{i}", [128, 1024], BF16))
               for i in range(2)]
        ptr = [p0ctx.enter_context(nc.psum_tensor(f"ptr{i}", [128, 1024], BF16))
               for i in range(2)]

        jobs = []
        for hc in range(W1T):
            jobs.append((w1_d[hc * 128:(hc + 1) * 128, :], D, c1_d, hc, 0))
        for hc in range(W1T):
            jobs.append((w2_d[hc * 128:(hc + 1) * 128, :], D, c2_d, hc, 0))
        for dc in range(W3T):
            for half in range(2):
                jobs.append((w3_d[dc * 128:(dc + 1) * 128,
                                  half * 5504:(half + 1) * 5504], 5504,
                             c3_d, dc, half * 43))

        wl_sems = ["wl_a", "wl_b"]
        dqs_sems = ["dqs_a", "dqs_b"]
        v_after_div = {}
        v_comb = {}
        v_after_evac = {}
        t_tile_last = {}
        gidx = 0

        for ji, (src_ap, F, dst_ap, c0, h0) in enumerate(jobs):
            slot = ji % 2
            if ji >= 2:
                B.wait("sync", "v", v_after_div[ji - 2])
            B.emit("sync",
                   lambda e, src_ap=src_ap, F=F, slot=slot:
                   e.dma_start(out=wt[slot][:, 0:F], in_=src_ap),
                   inc=(wl_sems[slot], 16))
            wl_tot = B.cnt[wl_sems[slot]]
            nbk = F // 64
            wv = wt[slot][:, 0:F].rearrange("p (nb e) -> p nb e", e=64)
            amv = am_t[:, 0:nbk]
            amo = am_t[:, 0:nbk].rearrange("p (nb one) -> p nb one", one=1)
            B.wait("vector", wl_sems[slot], wl_tot)
            B.emit("vector", lambda e, wv=wv, amo=amo: e.tensor_reduce(
                out=amo, in_=wv, axis=AX.X, op=ALU.max,
                apply_absolute_value=True))
            B.drain("vector")
            B.emit("vector", lambda e, amv=amv, nbk=nbk: e.tensor_scalar(
                out=eq_t[:, 0:nbk], in0=amv, scalar1=0.0, scalar2=None,
                op0=ALU.is_equal))
            B.drain("vector")
            B.emit("vector", lambda e, amv=amv, nbk=nbk: e.tensor_add(
                ag_t[:, 0:nbk], amv, eq_t[:, 0:nbk]))
            B.drain("vector")
            B.emit("vector", lambda e, nbk=nbk: e.reciprocal(
                out=eq_t[:, 0:nbk], in_=ag_t[:, 0:nbk]))
            B.drain("vector")
            rgb = eq_t[:, 0:nbk].unsqueeze(2).broadcast_to([128, nbk, 64])
            agb = ag_t[:, 0:nbk].unsqueeze(2).broadcast_to([128, nbk, 64])
            nv = nb_t[:, 0:F].rearrange("p (nb e) -> p nb e", e=64)
            B.emit("vector", lambda e, wv=wv, rgb=rgb, nv=nv: e.tensor_tensor(
                out=nv, in0=wv, in1=rgb, op=ALU.mult), inc=("v", 1))
            v_after_div[ji] = B.cnt["v"]
            B.drain("vector")
            nfl = nb_t[:, 0:F]
            B.emit("vector", lambda e, nfl=nfl, F=F: e.tensor_scalar(
                out=acc_t[:, 0:F], in0=nfl,
                scalar1=float(NF4_BOUNDS[0]), scalar2=float(NF4_DELTA[0]),
                op0=ALU.is_gt, op1=ALU.mult))
            B.drain("vector")
            for b in range(1, 15):
                B.emit("vector", lambda e, nfl=nfl, F=F, b=b: e.tensor_scalar(
                    out=pl_t[:, 0:F], in0=nfl,
                    scalar1=float(NF4_BOUNDS[b]), scalar2=float(NF4_DELTA[b]),
                    op0=ALU.is_gt, op1=ALU.mult))
                B.drain("vector")
                B.emit("vector", lambda e, F=F: e.tensor_add(
                    acc_t[:, 0:F], acc_t[:, 0:F], pl_t[:, 0:F]))
                B.drain("vector")
            if ji >= 2:
                B.wait("vector", "t", t_tile_last[ji - 2])
            dqv = dq_t[slot][:, 0:F].rearrange("p (nb e) -> p nb e", e=64)
            accv = acc_t[:, 0:F].rearrange("p (nb e) -> p nb e", e=64)
            B.emit("vector", lambda e, accv=accv, agb=agb, dqv=dqv:
                   e.scalar_tensor_tensor(out=dqv, in0=accv, scalar=-1.0,
                                          in1=agb, op0=ALU.add, op1=ALU.mult),
                   inc=("v", 1))
            v_dq = B.cnt["v"]
            B.drain("vector")
            ngroups = [(g * 8, 8) for g in range(F // 1024)]
            if F % 1024:
                ngroups.append(((F // 1024) * 8, (F % 1024) // 128))
            B.wait("tensor", "v", v_dq)
            for (k0, ng) in ngroups:
                par = gidx % 2
                if gidx >= 2:
                    B.wait("tensor", "v", v_after_evac[gidx - 2])
                for j in range(ng):
                    B.emit("tensor",
                           lambda e, k0=k0, j=j, par=par, slot=slot:
                           e.transpose(
                               out=ptr[par][:, j * 128:(j + 1) * 128],
                               in_=dq_t[slot][:, (k0 + j) * 128:
                                              (k0 + j + 1) * 128],
                               identity=identb[:]),
                           inc=("t", 1) if j == ng - 1 else None)
                t_grp = B.cnt["t"]
                ssl = gidx % 2
                B.wait("vector", "t", t_grp)
                B.wait("vector", dqs_sems[ssl], B.cnt[dqs_sems[ssl]])
                B.emit("vector", lambda e, ng=ng, par=par, ssl=ssl:
                       e.tensor_copy(out=stg[ssl][:, 0:ng * 128],
                                     in_=ptr[par][:, 0:ng * 128]),
                       inc=("v", 1))
                B.drain("vector")
                v_after_evac[gidx] = B.cnt["v"]
                dst = dst_ap[c0].rearrange("p (kk h) -> p kk h", h=128)[
                    :, h0 + k0:h0 + k0 + ng, :]
                B.wait("sync", "v", v_after_evac[gidx])
                B.emit("sync", lambda e, dst=dst, ng=ng, ssl=ssl: e.dma_start(
                    out=dst,
                    in_=stg[ssl][:, 0:ng * 128].rearrange(
                        "p (kk h) -> p kk h", h=128)),
                    inc=(dqs_sems[ssl], 16))
                gidx += 1
            t_tile_last[ji] = B.cnt["t"]
            if debug and ji == 0:
                B.wait("sync", "v", v_dq)
                B.emit("sync", lambda e: e.dma_start(out=dbg["dq1"],
                                                     in_=dq_t[0][:, 0:D]),
                       inc=("dbgs", 16))

        p0_dqs_a = B.cnt["dqs_a"]
        p0_dqs_b = B.cnt["dqs_b"]

    B.wait("gpsimd", "dqs_a", p0_dqs_a)
    B.wait("gpsimd", "dqs_b", p0_dqs_b)
    grp = [list(range(ncores))]
    for (cin, cout) in ((c1_d, dq1_d), (c2_d, dq2_d), (c3_d, dq3_d)):
        B.emit("gpsimd", lambda e, cin=cin, cout=cout: e.collective_compute(
            "AllGather", ALU.bypass, replica_groups=grp,
            ins=[cin.rearrange("t p c -> (t p) c")],
            outs=[cout.rearrange("t p c -> (t p) c")]),
            inc=("cc", 1))
    p0_cc = B.cnt["cc"]

    if "1" in phases:
        for eng in ENGINES:
            B.wait(eng, "dqs_a", p0_dqs_a)
            B.wait(eng, "dqs_b", p0_dqs_b)
            if debug:
                B.wait(eng, "dbgs")

        with ExitStack() as p1ctx:
            xb = p1ctx.enter_context(nc.sbuf_tensor("xb", [128, D], F32))
            castC = [p1ctx.enter_context(
                nc.sbuf_tensor(f"castC{i}", [128, D], BF16))
                for i in range(2)]
            ln1fm = p1ctx.enter_context(
                nc.sbuf_tensor("ln1fm", [128, NKC * 512], BF16))
            st_t = p1ctx.enter_context(nc.sbuf_tensor("lnst", [128, 16], F32))
            pfm = p1ctx.enter_context(
                nc.sbuf_tensor("pfm", [128, NFC * 512], BF16))
            l2st0 = p1ctx.enter_context(
                nc.sbuf_tensor("l2st0", [128, 1024], BF16))
            l2st = [l2st0, l2st0]
            top8 = [p1ctx.enter_context(nc.sbuf_tensor(
                f"top8_{i}", [128, 8], F32)) for i in range(2)]
            idx8u = [p1ctx.enter_context(nc.sbuf_tensor(
                f"idx8u_{i}", [128, 8], U32)) for i in range(2)]
            z8 = [p1ctx.enter_context(nc.sbuf_tensor(
                f"z8_{i}", [128, 8], F32)) for i in range(2)]
            e8 = [p1ctx.enter_context(nc.sbuf_tensor(
                f"e8_{i}", [128, 8], F32)) for i in range(2)]
            w8 = [p1ctx.enter_context(nc.sbuf_tensor(
                f"w8_{i}", [128, 8], F32)) for i in range(2)]
            d32 = [p1ctx.enter_context(nc.sbuf_tensor(
                f"d32_{i}", [128, 8], F32)) for i in range(2)]
            fidx = [p1ctx.enter_context(nc.sbuf_tensor(
                f"fidx_{i}", [128, 8], F32)) for i in range(2)]
            gidxu = [p1ctx.enter_context(nc.sbuf_tensor(
                f"gidxu_{i}", [128, 8], U32)) for i in range(2)]
            dfr = [p1ctx.enter_context(
                nc.sbuf_tensor(f"dfr{i}", [128, 2 * NKC * 128], BF16))
                for i in range(2)]
            sqa = p1ctx.enter_context(nc.sbuf_tensor("sqa", [128, 512], BF16))
            sqb = p1ctx.enter_context(nc.sbuf_tensor("sqb", [128, 512], BF16))
            cosb = [p1ctx.enter_context(
                nc.sbuf_tensor(f"cosb{i}", [128, 2048], BF16))
                for i in range(2)]
            cx = p1ctx.enter_context(nc.sbuf_tensor("cx", [128, D], F32))
            shb = [p1ctx.enter_context(
                nc.sbuf_tensor(f"shb{i}", [128, D], BF16)) for i in range(4)]
            acb = [p1ctx.enter_context(
                nc.sbuf_tensor(f"acb{i}", [128, D], F32)) for i in range(2)]
            cps4 = [p1ctx.enter_context(
                nc.psum_tensor(f"cps{i}", [128, 512], F32)) for i in range(4)]
            breB = p1ctx.enter_context(nc.psum_tensor("breB", [128, 512], F32))
            bimB = p1ctx.enter_context(nc.psum_tensor("bimB", [128, 512], F32))
            ptrP = [p1ctx.enter_context(
                nc.psum_tensor(f"ptrP{i}", [128, 1024], BF16))
                for i in range(2)]

            df_sems = ["df_a", "df_b"]
            cs_sems = ["cs_a", "cs_b"]
            gs_sems = ["gs_a", "gs_b", "gs_c", "gs_d"]
            sl2_sems = ["sl2_a", "sl2_a"]
            gtile = 0
            ev_p1 = {}
            xb_credit = [None]
            v_acc7 = {}
            a_ln2_S = {}

            def emit_transposes(src_bf, dst_fn, to_dram):
                nonlocal gtile
                for g in range(4):
                    par = gtile % 2
                    if gtile >= 2:
                        B.wait("tensor", "v", ev_p1[gtile - 2])
                    for j in range(8):
                        B.emit("tensor", lambda e, g=g, j=j, par=par:
                               e.transpose(
                                   out=ptrP[par][:, j * 128:(j + 1) * 128],
                                   in_=src_bf[:, (g * 8 + j) * 128:
                                              (g * 8 + j + 1) * 128],
                                   identity=identb[:]),
                               inc=("t", 1) if j == 7 else None)
                    t_tot = B.cnt["t"]
                    B.wait("vector", "t", t_tot)
                    if to_dram:
                        B.wait("vector", "sl2_a", B.cnt["sl2_a"])
                        B.emit("vector", lambda e, par=par:
                               e.tensor_copy(out=l2st0[:],
                                             in_=ptrP[par][:]),
                               inc=("v", 1))
                        B.drain("vector")
                        ev_p1[gtile] = B.cnt["v"]
                        dst = dst_fn(g)
                        B.wait("sync", "v", ev_p1[gtile])
                        B.emit("sync", lambda e, dst=dst:
                               e.dma_start(
                                   out=dst, in_=l2st0[:].rearrange(
                                       "p (kk h) -> p kk h", h=128)),
                               inc=("sl2_a", 16))
                    else:
                        dst = dst_fn(g)
                        B.emit("vector", lambda e, par=par, dst=dst:
                               e.tensor_copy(
                                   out=dst, in_=ptrP[par][:].rearrange(
                                       "p (kk h) -> p kk h", h=128)),
                               inc=("v", 1))
                        B.drain("vector")
                        ev_p1[gtile] = B.cnt["v"]
                    gtile += 1

            def _p1_tile(tt):
                for s in range(4):
                    S = tt * 4 + s
                    if xb_credit[0] is not None:
                        B.wait("sync", *xb_credit[0])
                    B.emit("sync", lambda e, tt=tt, s=s:
                           e.dma_start(
                               out=xb[:],
                               in_=x_d[tt * 512 + s * 128:
                                       tt * 512 + (s + 1) * 128, :]),
                           inc=("xi_a", 16))
                    xi_tot = B.cnt["xi_a"]
                    B.wait("vector", "xi_a", xi_tot)
                    B.wait("scalar", "xi_a", xi_tot)
                    B.wait("scalar", "t")
                    B.wait("scalar", "gv", B.cnt["gv"])
                    a_norm = _ln_emit(B, xb[:], st_t[:], [castC[s % 2][:]])
                    xb_credit[0] = ("a", a_norm)
                    B.wait("gpsimd", "a", a_norm)
                    for nm in gs_sems:
                        B.wait("gpsimd", nm, B.cnt[nm])
                    B.emit("gpsimd", lambda e, s=s: e.dma_start(
                        out=vv_d[s][:, 0:D], in_=castC[s % 2][:]),
                        inc=("gv", 16))
                    B.emit("gpsimd", lambda e, s=s: e.dma_start(
                        out=vv_d[s][:, D:2 * D], in_=castC[s % 2][:]),
                        inc=("gv", 16))
                    if debug and tt == 0 and s == 0:
                        B.wait("sync", "a", a_norm)
                        B.emit("sync", lambda e, s=s: e.dma_start(
                            out=dbg["ln1"], in_=castC[s % 2][:]),
                            inc=("dbgs", 16))
                    B.wait("tensor", "a", a_norm)

                    def _dst_ln1(g, s=s):
                        return ln1fm[:].rearrange(
                            "p (kc t) -> p kc t", t=512)[
                            :, g * 8:(g + 1) * 8, s * 128:(s + 1) * 128]
                    emit_transposes(castC[s % 2], _dst_ln1, False)
                v_ln1fm_done = B.cnt["v"]

                B.wait("tensor", "v", v_ln1fm_done)
                B.wait("tensor", "a")
                B.wait("scalar", "v")
                B.wait("scalar", "sx1")
                v_p_done = {}
                t_dft_fc = {}
                a_sq_fc = {}
                for fc in range(NFC):
                    par = fc % 2
                    if fc >= 2:
                        B.wait("scalar", "t", t_dft_fc[fc - 2])
                    B.emit("scalar", lambda e, fc=fc, par=par: e.dma_start(
                        out=dfr[par][:, 0:NKC * 128], in_=dftre_d[fc]),
                        inc=(df_sems[par], 16))
                    B.emit("scalar", lambda e, fc=fc, par=par: e.dma_start(
                        out=dfr[par][:, NKC * 128:2 * NKC * 128],
                        in_=dftim_d[fc]),
                        inc=(df_sems[par], 16))
                    df_tot = B.cnt[df_sems[par]]
                    B.wait("tensor", df_sems[par], df_tot)
                    if fc >= 1:
                        B.wait("tensor", "a", a_sq_fc[fc - 1])
                    for kc in range(NKC):
                        B.emit("tensor", lambda e, kc=kc, par=par: e.matmul(
                            breB[:],
                            lhsT=dfr[par][:, kc * 128:(kc + 1) * 128],
                            rhs=ln1fm[:, kc * 512:(kc + 1) * 512],
                            start=(kc == 0), stop=(kc == NKC - 1)))
                    for kc in range(NKC):
                        B.emit("tensor", lambda e, kc=kc, par=par: e.matmul(
                            bimB[:],
                            lhsT=dfr[par][:, NKC * 128 + kc * 128:
                                          NKC * 128 + (kc + 1) * 128],
                            rhs=ln1fm[:, kc * 512:(kc + 1) * 512],
                            start=(kc == 0), stop=(kc == NKC - 1)),
                            inc=("t", 1) if kc == NKC - 1 else None)
                    t_dft_fc[fc] = B.cnt["t"]
                    B.wait("scalar", "t", t_dft_fc[fc])
                    if fc >= 1:
                        B.wait("scalar", "v", v_p_done[fc - 1])
                    B.drain("scalar")
                    B.emit("scalar", lambda e: e.activation(
                        out=sqa[:], in_=breB[:], func=AF.Square))
                    B.drain("scalar")
                    B.emit("scalar", lambda e: e.activation(
                        out=sqb[:], in_=bimB[:], func=AF.Square),
                        inc=("a", 1))
                    B.drain("scalar")
                    a_sq_fc[fc] = B.cnt["a"]
                    B.wait("vector", "a", a_sq_fc[fc])
                    B.emit("vector", lambda e, fc=fc: e.tensor_add(
                        pfm[:, fc * 512:(fc + 1) * 512], sqa[:], sqb[:]),
                        inc=("v", 1))
                    B.drain("vector")
                    v_p_done[fc] = B.cnt["v"]

                def _part_a(s, cgate):
                    S = tt * 4 + s
                    ss = s % 2
                    v_evac = None
                    for half in range(2):
                        ntc = 4 if half == 0 else 1
                        cs_t = {}
                        for fc in range(NFC):
                            par = fc % 2
                            if fc >= 2:
                                B.wait("scalar", "t", cs_t[fc - 2])
                            else:
                                B.wait("scalar", "t")
                            B.emit("scalar", lambda e, fc=fc, half=half,
                                   par=par, ntc=ntc: e.dma_start(
                                       out=cosb[par][:, 0:ntc * 512],
                                       in_=cosm_d[fc][:, half * 2048:
                                                      half * 2048 +
                                                      ntc * 512]),
                                   inc=(cs_sems[par], 16))
                            cs_tot = B.cnt[cs_sems[par]]
                            B.wait("tensor", cs_sems[par], cs_tot)
                            if fc == 0:
                                if half == 0 and cgate is None:
                                    B.wait("tensor", "a")
                                    B.wait("tensor", "v")
                                elif half == 0:
                                    B.wait("tensor", "v",
                                           max(cgate,
                                               v_p_done[NFC - 1]))
                                else:
                                    B.wait("tensor", "v", v_evac)
                            for tc in range(ntc):
                                B.emit("tensor", lambda e, fc=fc, s=s,
                                       tc=tc, par=par: e.matmul(
                                           cps4[tc][:],
                                           lhsT=pfm[:, fc * 512 + s * 128:
                                                    fc * 512 +
                                                    (s + 1) * 128],
                                           rhs=cosb[par][:, tc * 512:
                                                         (tc + 1) * 512],
                                           start=(fc == 0),
                                           stop=(fc == NFC - 1)),
                                       inc=("t", 1) if tc == ntc - 1
                                       else None)
                            cs_t[fc] = B.cnt["t"]
                        t_c = B.cnt["t"]
                        B.wait("vector", "t", t_c)
                        for tc in range(ntc):
                            B.emit("vector", lambda e, tc=tc, half=half:
                                   e.tensor_copy(
                                       out=cx[:, half * 2048 + tc * 512:
                                              half * 2048 + (tc + 1) * 512],
                                       in_=cps4[tc][:]),
                                   inc=("v", 1) if tc == ntc - 1 else None)
                            B.drain("vector")
                        v_evac = B.cnt["v"]
                    B.emit("vector", lambda e: e.tensor_copy(
                        out=cx[:, 2560:4096],
                        in_=bass.AP(cx, 1536, [[4096, 128], [-1, 1536]])),
                        inc=("v", 1))
                    B.drain("vector")
                    v_c = B.cnt["v"]
                    if debug and tt == 0 and s == 0:
                        B.wait("sync", "v", v_c)
                        B.emit("sync", lambda e: e.dma_start(out=dbg["c"],
                                                             in_=cx[:]),
                               inc=("dbgs", 16))
                    B.emit("vector", lambda e, ss=ss: e.max(
                        out=top8[ss][:], in_=cx[:]))
                    B.drain("vector")
                    B.emit("vector", lambda e, ss=ss: e.max_index(
                        out=idx8u[ss][:], in_max=top8[ss][:],
                        in_values=cx[:]))
                    B.drain("vector")
                    B.emit("vector", lambda e, ss=ss: e.tensor_scalar(
                        out=z8[ss][:], in0=top8[ss][:],
                        scalar1=top8[ss][:, 0:1],
                        scalar2=-80.0, op0=ALU.subtract, op1=ALU.max),
                        inc=("v", 1))
                    B.drain("vector")
                    v_z = B.cnt["v"]
                    B.wait("scalar", "v", v_z)
                    B.drain("scalar")
                    B.emit("scalar", lambda e, ss=ss: e.activation(
                        out=e8[ss][:], in_=z8[ss][:], func=AF.Exp),
                        inc=("a", 1))
                    B.drain("scalar")
                    a_e = B.cnt["a"]
                    B.wait("vector", "a", a_e)
                    B.emit("vector", lambda e, ss=ss: e.reduce_sum(
                        out=st_t[:, 14:15], in_=e8[ss][:], axis=AX.X))
                    B.drain("vector")
                    B.emit("vector", lambda e: e.reciprocal(
                        out=st_t[:, 15:16], in_=st_t[:, 14:15]))
                    B.drain("vector")
                    B.emit("vector", lambda e, ss=ss: e.tensor_scalar(
                        out=w8[ss][:], in0=e8[ss][:],
                        scalar1=st_t[:, 15:16],
                        scalar2=None, op0=ALU.mult))
                    B.drain("vector")
                    B.emit("vector", lambda e, ss=ss: e.tensor_copy(
                        out=d32[ss][:], in_=idx8u[ss][:]))
                    B.drain("vector")
                    B.emit("vector", lambda e, ss=ss: e.tensor_add(
                        fidx[ss][:], d32[ss][:],
                        rowb[:].broadcast_to([128, 8])))
                    B.drain("vector")
                    B.emit("vector", lambda e, ss=ss: e.tensor_copy(
                        out=gidxu[ss][:], in_=fidx[ss][:]),
                        inc=("v", 1))
                    B.drain("vector")
                    v_idx = B.cnt["v"]
                    if debug and tt == 0 and s == 0:
                        B.wait("sync", "v", v_idx)
                        B.emit("sync", lambda e, ss=ss: e.dma_start(
                            out=dbg["top8"], in_=top8[ss][:]),
                            inc=("dbgs", 16))
                        B.emit("sync", lambda e, ss=ss: e.dma_start(
                            out=dbg["idx8"], in_=idx8u[ss][:]),
                            inc=("dbgs", 16))
                    vv_flat = vv_d[s].rearrange("p f -> (p f)").unsqueeze(1)
                    B.wait("gpsimd", "v", v_idx)
                    B.wait("gpsimd", "gv", B.cnt["gv"])
                    if S >= 2:
                        B.wait("gpsimd", "v", v_acc7[S - 2])
                    gs_tots = []
                    for k in range(2):
                        gp = 2 * ss + (k % 2)
                        B.emit("gpsimd",
                               lambda e, k=k, gp=gp, ss=ss,
                               vv_flat=vv_flat:
                               e.indirect_dma_start(
                                   out=shb[gp][:], out_offset=None,
                                   in_=vv_flat,
                                   in_offset=IndirectOffsetOnAxis(
                                       gidxu[ss][:, k:k + 1], 0)),
                               inc=(gs_sems[gp], 16))
                        gs_tots.append(B.cnt[gs_sems[gp]])
                    return v_c, (gs_tots, vv_flat)

                def _part_b(s, ga):
                    gs_tots, vv_flat = ga
                    S = tt * 4 + s
                    ss = s % 2
                    if S >= 1 and (S - 1) in a_ln2_S:
                        B.wait("vector", "a", a_ln2_S[S - 1])
                    B.wait("vector", "sx1")
                    v_acc = {}
                    for k in range(8):
                        gp = 2 * ss + (k % 2)
                        if k >= 2:
                            B.wait("gpsimd", "v", v_acc[k - 2])
                            B.emit("gpsimd",
                                   lambda e, k=k, gp=gp, ss=ss,
                                   vv_flat=vv_flat:
                                   e.indirect_dma_start(
                                       out=shb[gp][:], out_offset=None,
                                       in_=vv_flat,
                                       in_offset=IndirectOffsetOnAxis(
                                           gidxu[ss][:, k:k + 1], 0)),
                                   inc=(gs_sems[gp], 16))
                            gs_tots.append(B.cnt[gs_sems[gp]])
                        B.wait("vector", gs_sems[gp], gs_tots[k])
                        if k == 0:
                            B.emit("vector", lambda e, gp=gp, ss=ss:
                                   e.tensor_scalar(
                                       out=acb[0][:], in0=shb[gp][:],
                                       scalar1=w8[ss][:, 0:1], scalar2=None,
                                       op0=ALU.mult), inc=("v", 1))
                        else:
                            B.emit("vector", lambda e, k=k, gp=gp, ss=ss:
                                   e.scalar_tensor_tensor(
                                       out=acb[k % 2][:], in0=shb[gp][:],
                                       scalar=w8[ss][:, k:k + 1],
                                       in1=acb[(k - 1) % 2][:],
                                       op0=ALU.mult, op1=ALU.add),
                                   inc=("v", 1))
                        B.drain("vector")
                        v_acc[k] = B.cnt["v"]
                    v_acc7[S] = v_acc[7]
                    attn_ap = acb[1][:]
                    if debug and tt == 0 and s == 0:
                        B.wait("sync", "v", v_acc[7])
                        B.emit("sync", lambda e: e.dma_start(
                            out=dbg["attn"], in_=attn_ap), inc=("dbgs", 16))
                    B.wait("vector", "v", v_acc[7])
                    B.wait("scalar", "v", v_acc[7])
                    _ln_emit(B, attn_ap, st_t[:], [acb[0][:]])
                    a_lnat = B.cnt["a"]
                    B.wait("sync", "a", a_lnat)
                    B.emit("sync", lambda e, tt=tt, s=s: e.dma_start(
                        out=acb[1][:],
                        in_=x_d[tt * 512 + s * 128:
                                tt * 512 + (s + 1) * 128, :]),
                        inc=("xi_b", 16))
                    xr_tot = B.cnt["xi_b"]
                    B.wait("vector", "a", a_lnat)
                    B.wait("vector", "xi_b", xr_tot)
                    B.emit("vector", lambda e: e.tensor_add(
                        acb[0][:], acb[0][:], acb[1][:]), inc=("v", 1))
                    B.drain("vector")
                    v_x1 = B.cnt["v"]
                    if debug and tt == 0 and s == 0:
                        B.wait("sync", "v", v_x1)
                        B.emit("sync", lambda e: e.dma_start(
                            out=dbg["x1"], in_=acb[0][:]), inc=("dbgs", 16))
                    B.wait("sync", "v", v_x1)
                    B.emit("sync", lambda e, tt=tt, s=s: e.dma_start(
                        out=x1_d[tt * 512 + s * 128:
                                 tt * 512 + (s + 1) * 128, :],
                        in_=acb[0][:]), inc=("sx1", 16))
                    B.wait("scalar", "v", v_x1)
                    B.wait("scalar", "t")
                    B.wait("scalar", "gv", B.cnt["gv"])
                    _ln_emit(B, acb[0][:], st_t[:], [castC[s % 2][:]])
                    a_ln2_S[S] = B.cnt["a"]
                    return a_ln2_S[S]

                def _tr(s, a_ln2):
                    B.wait("tensor", "a", a_ln2)

                    def _dst_ln2(g, tt=tt, s=s):
                        return ln2_d.rearrange(
                            "(kc p) t -> p kc t", p=128)[
                            :, g * 8:(g + 1) * 8,
                            tt * 512 + s * 128: tt * 512 + (s + 1) * 128]
                    emit_transposes(castC[s % 2], _dst_ln2, True)

                return _part_a, _part_b, _tr

            _cg = None
            for _tt in range(NTILE):
                _pa, _pb, _ptr = _p1_tile(_tt)
                _ga = {}
                _cg, _ga[0] = _pa(0, _cg)
                _cg, _ga[1] = _pa(1, _cg)
                _a2 = _pb(0, _ga[0])
                _cg, _ga[2] = _pa(2, _cg)
                _a2b = _pb(1, _ga[1])
                _ptr(0, _a2)
                _cg, _ga[3] = _pa(3, _cg)
                _a2c = _pb(2, _ga[2])
                _ptr(1, _a2b)
                _a2d = _pb(3, _ga[3])
                _ptr(2, _a2c)
                _ptr(3, _a2d)

            p1_sx1 = B.cnt["sx1"]
            p1_sl2a = B.cnt["sl2_a"]
            p1_sl2b = B.cnt["sl2_b"]

    for eng in ENGINES:
        B.wait(eng, "sx1", B.cnt["sx1"])
        B.wait(eng, "sl2_a", B.cnt["sl2_a"])
        B.wait(eng, "sl2_b", B.cnt["sl2_b"])
        B.wait(eng, "gv")
        B.wait(eng, "gx")
        B.wait(eng, "gs_a")
        B.wait(eng, "gs_b")
        B.wait(eng, "gs_c")
        B.wait(eng, "gs_d")
        if debug:
            B.wait(eng, "dbgs")

    if "2" in phases:
        for eng in ENGINES:
            B.wait(eng, "sx1", B.cnt["sx1"])
            B.wait(eng, "sl2_a", B.cnt["sl2_a"])
            B.wait(eng, "sl2_b", B.cnt["sl2_b"])
            B.wait(eng, "gv")
            B.wait(eng, "gs_a")
            B.wait(eng, "gs_b")
            if debug:
                B.wait(eng, "dbgs")

        with ExitStack() as p2ctx:
            actb = p2ctx.enter_context(
                nc.sbuf_tensor("actb", [128, NHC * 512], BF16))
            st2 = p2ctx.enter_context(nc.sbuf_tensor("st2", [128, 16], F32))
            ytsb = p2ctx.enter_context(nc.sbuf_tensor("ytsb", [128, D], F32))
            outb = [p2ctx.enter_context(nc.sbuf_tensor(
                f"outb{i}", [128, D], F32)) for i in range(2)]
            pg = [p2ctx.enter_context(
                nc.psum_tensor(f"pg{i}", [128, 512], F32)) for i in range(2)]
            pu = [p2ctx.enter_context(
                nc.psum_tensor(f"pu{i}", [128, 512], F32)) for i in range(2)]
            pyf = [p2ctx.enter_context(
                nc.psum_tensor(f"pyf{i}", [128, 512], F32)) for i in range(2)]
            pyt = [p2ctx.enter_context(
                nc.psum_tensor(f"pyt{i}", [128, 512], F32)) for i in range(2)]

            w12_sems = ["w12_a", "w12_b"]
            d3_sems = ["d3_a", "d3_b"]
            gy_sems = ["gy_a", "gy_b"]
            so_sems = ["so_a", "so_b"]

            B.wait("sync", "dqs_a", p0_dqs_a)
            B.wait("sync", "dqs_b", p0_dqs_b)
            B.wait("sync", "sl2_a", B.cnt["sl2_a"])
            B.wait("sync", "sl2_b", B.cnt["sl2_b"])
            B.wait("sync", "sx1", B.cnt["sx1"])
            B.wait("sync", "cc", p0_cc)
            B.wait("scalar", "cc", p0_cc)
            B.wait("scalar", "dqs_a", p0_dqs_a)
            B.wait("scalar", "dqs_b", p0_dqs_b)

            def _p2_tile(tt):
                with ExitStack() as m12ctx:
                    l2s = m12ctx.enter_context(nc.sbuf_tensor(
                        f"l2s{tt}", [128, NKC * 512], BF16))
                    w1s = [m12ctx.enter_context(nc.sbuf_tensor(
                        f"w1s{tt}_{i}", [128, NKC * 128], BF16))
                        for i in range(2)]
                    w2s = [m12ctx.enter_context(nc.sbuf_tensor(
                        f"w2s{tt}_{i}", [128, NKC * 128], BF16))
                        for i in range(2)]
                    sgs = [m12ctx.enter_context(nc.sbuf_tensor(
                        f"sgs{tt}_{i}", [128, 512], F32)) for i in range(2)]

                    l2dst = l2s[:].rearrange("p (kc t) -> p kc t", t=512)
                    l2src = ln2_d.rearrange("(kc p) t -> p kc t", p=128)[
                        :, :, tt * 512:(tt + 1) * 512]
                    B.wait("sync", "t")
                    B.wait("sync", "gy_a", B.cnt["gy_a"])
                    B.wait("sync", "gy_b", B.cnt["gy_b"])
                    B.wait("vector", "a")
                    B.emit("sync", lambda e, l2dst=l2dst, l2src=l2src:
                           e.dma_start(out=l2dst, in_=l2src), inc=("l2l", 16))
                    l2_tot = B.cnt["l2l"]
                    B.wait("tensor", "l2l", l2_tot)
                    B.wait("tensor", "v")

                    t_mm = {}
                    v_act = {}
                    w1_sems = ["w12_a", "w12_b"]
                    w2_sems = ["w2_a", "w2_b"]
                    for hc in range(NHC):
                        par = hc % 2
                        sl3 = hc % 2
                        if hc >= 2:
                            B.wait("sync", "t", t_mm[hc - 2])
                        B.emit("sync", lambda e, hc=hc, sl3=sl3: e.dma_start(
                            out=w1s[sl3][:], in_=dq1_d[hc]),
                            inc=(w1_sems[sl3], 16))
                        B.emit("sync", lambda e, hc=hc, par=par:
                               e.dma_start(
                                   out=w2s[par][:], in_=dq2_d[hc]),
                               inc=(w2_sems[par], 16))
                        w1_tot = B.cnt[w1_sems[sl3]]
                        w2_tot = B.cnt[w2_sems[par]]
                        B.wait("tensor", w1_sems[sl3], w1_tot)
                        B.wait("tensor", w2_sems[par], w2_tot)
                        if hc >= 2:
                            B.wait("tensor", "v", v_act[hc - 2])
                        for kc in range(NKC):
                            B.emit("tensor", lambda e, kc=kc, par=par,
                                   sl3=sl3: e.matmul(
                                pg[par][:],
                                lhsT=w1s[sl3][:, kc * 128:(kc + 1) * 128],
                                rhs=l2s[:, kc * 512:(kc + 1) * 512],
                                start=(kc == 0), stop=(kc == NKC - 1)))
                        for kc in range(NKC):
                            B.emit("tensor", lambda e, kc=kc, par=par,
                                   sl3=sl3: e.matmul(
                                pu[par][:],
                                lhsT=w2s[par][:, kc * 128:(kc + 1) * 128],
                                rhs=l2s[:, kc * 512:(kc + 1) * 512],
                                start=(kc == 0), stop=(kc == NKC - 1)),
                                inc=("t", 1) if kc == NKC - 1 else None)
                        t_mm[hc] = B.cnt["t"]
                        B.wait("scalar", "t", t_mm[hc])
                        if hc >= 2:
                            B.wait("scalar", "v", v_act[hc - 2])
                        B.drain("scalar")
                        B.emit("scalar", lambda e, par=par: e.activation(
                            out=sgs[par][:], in_=pg[par][:], func=AF.Silu),
                            inc=("a", 1))
                        B.drain("scalar")
                        a_si = B.cnt["a"]
                        B.wait("vector", "a", a_si)
                        B.emit("vector", lambda e, hc=hc, par=par:
                               e.tensor_tensor(
                                   out=actb[:, hc * 512:(hc + 1) * 512],
                                   in0=sgs[par][:], in1=pu[par][:],
                                   op=ALU.mult), inc=("v", 1))
                        B.drain("vector")
                        v_act[hc] = B.cnt["v"]
                    v_act_done = B.cnt["v"]
                    B.wait("tensor", "v", v_act_done)

                with ExitStack() as m3ctx:
                    d3s = [m3ctx.enter_context(nc.sbuf_tensor(
                        f"d3s{tt}_{i}", [128, NHC * 128], BF16))
                        for i in range(2)]
                    yst = [m3ctx.enter_context(nc.sbuf_tensor(
                        f"yst{tt}_{i}", [128, 512], F32)) for i in range(2)]
                    ytst = [m3ctx.enter_context(nc.sbuf_tensor(
                        f"ytst{tt}_{i}", [128, 512], F32)) for i in range(2)]

                    t_m3 = {}
                    v_yev = {}
                    t_ytr = {}
                    v_ytev = {}
                    for dc in range(NKC):
                        par = dc % 2
                        if dc >= 2:
                            B.wait("scalar", "t", t_m3[dc - 2])
                        else:
                            B.wait("scalar", "v", v_act_done)
                        B.emit("scalar", lambda e, dc=dc, par=par: e.dma_start(
                            out=d3s[par][:], in_=dq3_d[dc]),
                            inc=(d3_sems[par], 16))
                        d3_tot = B.cnt[d3_sems[par]]
                        B.wait("tensor", d3_sems[par], d3_tot)
                        if dc >= 2:
                            B.wait("tensor", "v", v_yev[dc - 2])
                        for hc in range(NHC):
                            B.emit("tensor", lambda e, hc=hc, par=par: e.matmul(
                                pyf[par][:],
                                lhsT=d3s[par][:, hc * 128:(hc + 1) * 128],
                                rhs=actb[:, hc * 512:(hc + 1) * 512],
                                start=(hc == 0), stop=(hc == NHC - 1)),
                                inc=("t", 1) if hc == NHC - 1 else None)
                        t_m3[dc] = B.cnt["t"]
                        B.wait("vector", "t", t_m3[dc])
                        B.emit("vector", lambda e, par=par: e.tensor_copy(
                            out=yst[par][:], in_=pyf[par][:]), inc=("v", 1))
                        B.drain("vector")
                        v_yev[dc] = B.cnt["v"]
                        B.wait("tensor", "v", v_yev[dc])
                        if dc >= 2:
                            B.wait("tensor", "v", v_ytev[dc - 2])
                        for sj in range(4):
                            B.emit("tensor", lambda e, sj=sj, par=par: e.matmul(
                                pyt[par][:, sj * 128:(sj + 1) * 128],
                                lhsT=yst[par][:, sj * 128:(sj + 1) * 128],
                                rhs=identf[:], start=True, stop=True),
                                inc=("t", 1) if sj == 3 else None)
                        t_ytr[dc] = B.cnt["t"]
                        B.wait("vector", "t", t_ytr[dc])
                        B.wait("vector", gy_sems[par], B.cnt[gy_sems[par]])
                        B.emit("vector", lambda e, par=par: e.tensor_copy(
                            out=ytst[par][:], in_=pyt[par][:]), inc=("v", 1))
                        B.drain("vector")
                        v_ytev[dc] = B.cnt["v"]
                        ydst = ytm_d.rearrange("s p d -> p s d")[
                            :, :, dc * 128:(dc + 1) * 128]
                        B.wait("gpsimd", "v", v_ytev[dc])
                        B.emit("gpsimd", lambda e, ydst=ydst, par=par:
                               e.dma_start(
                                   out=ydst, in_=ytst[par][:].rearrange(
                                       "p (s d) -> p s d", d=128)),
                               inc=(gy_sems[par], 16))
                    gy_a_tot = B.cnt["gy_a"]
                    gy_b_tot = B.cnt["gy_b"]

                B.wait("sync", "gy_a", gy_a_tot)
                B.wait("sync", "gy_b", gy_b_tot)
                x1_sems = ["xi_a", "xi_b"]
                for s in range(4):
                    par = s % 2
                    B.wait("sync", "a")
                    B.emit("sync", lambda e, s=s: e.dma_start(
                        out=ytsb[:], in_=ytm_d[s]), inc=("ye_a", 16))
                    ye0 = B.cnt["ye_a"]
                    B.wait("sync", so_sems[par], B.cnt[so_sems[par]])
                    B.emit("sync", lambda e, tt=tt, s=s, par=par:
                           e.dma_start(
                               out=outb[par][:],
                               in_=x1_d[tt * 512 + s * 128:
                                        tt * 512 + (s + 1) * 128, :]),
                           inc=(x1_sems[par], 16))
                    ye1 = B.cnt[x1_sems[par]]
                    B.wait("vector", "ye_a", ye0)
                    B.wait("scalar", "ye_a", ye0)
                    _ln_emit(B, ytsb[:], st2[:], [ytsb[:]],
                             sqscr=actb[:, 0:D])
                    a_ln3 = B.cnt["a"]
                    B.wait("vector", "a", a_ln3)
                    B.wait("vector", x1_sems[par], ye1)
                    B.emit("vector", lambda e, par=par: e.tensor_add(
                        outb[par][:], outb[par][:], ytsb[:]), inc=("v", 1))
                    B.drain("vector")
                    v_out = B.cnt["v"]
                    if debug and tt == 0 and s == 0:
                        B.wait("sync", "v", v_out)
                        B.emit("sync", lambda e: e.dma_start(
                            out=dbg["yt"], in_=ytsb[:]), inc=("dbgs", 16))
                    B.wait("sync", "v", v_out)
                    B.emit("sync", lambda e, tt=tt, s=s, par=par:
                           e.dma_start(
                               out=out_d[tt * 512 + s * 128:
                                         tt * 512 + (s + 1) * 128, :],
                               in_=outb[par][:]), inc=(so_sems[par], 16))

            for _tt in range(NTILE):
                _p2_tile(_tt)

            B.wait("sync", "so_a", B.cnt["so_a"])
            B.wait("sync", "so_b", B.cnt["so_b"])
            if debug:
                B.wait("sync", "dbgs", B.cnt["dbgs"])

    for nm in ("dqs_a", "dqs_b", "sx1", "sl2_a", "sl2_b", "gv",
               "gs_a", "gs_b", "gs_c", "gs_d", "gy_a", "gy_b", "so_a",
               "so_b", "cc", "gx"):
        B.wait("sync", nm)
    B.finalize_block()
    B.ctx.close()
    return nc


_CACHE = {}


def kernel(L0_attn_x, L0_ffn_w1, L0_ffn_w2, L0_ffn_w3):
    x = np.ascontiguousarray(np.asarray(L0_attn_x, dtype=np.float32)
                             .reshape(16384, D))
    w1 = np.ascontiguousarray(np.asarray(L0_ffn_w1, dtype=np.float32))
    w2 = np.ascontiguousarray(np.asarray(L0_ffn_w2, dtype=np.float32))
    w3 = np.ascontiguousarray(np.asarray(L0_ffn_w3, dtype=np.float32))
    if "consts" not in _CACHE:
        _CACHE["consts"] = _build_consts()
    consts = _CACHE["consts"]
    T = 16384 // NCORES
    nc = build_nc(T=T)
    w1p = np.concatenate([w1, np.zeros((88 * 128 - H, D), np.float32)], 0)
    w2p = np.concatenate([w2, np.zeros((88 * 128 - H, D), np.float32)], 0)
    r1 = (88 // NCORES) * 128
    r3 = (NKC // NCORES) * 128
    in_maps = []
    for i in range(NCORES):
        m = {"x": x[i * T:(i + 1) * T],
             "w1s": np.ascontiguousarray(w1p[i * r1:(i + 1) * r1]),
             "w2s": np.ascontiguousarray(w2p[i * r1:(i + 1) * r1]),
             "w3s": np.ascontiguousarray(w3[i * r3:(i + 1) * r3])}
        m.update(consts)
        in_maps.append(m)
    res = run_bass_kernel_spmd(nc, in_maps, core_ids=list(range(NCORES)))
    out = np.concatenate([res.results[i]["out"] for i in range(NCORES)],
                         axis=0)
    return out.reshape(2, 8192, D)


# version 13
# speedup vs baseline: 1.4966x; 1.0731x over previous
from contextlib import ExitStack

import numpy as np
import ml_dtypes

import concourse.bass as bass
import concourse.mybir as mybir
from concourse.bass import IndirectOffsetOnAxis
from concourse.bass_utils import run_bass_kernel_spmd

F32 = mybir.dt.float32
BF16 = mybir.dt.bfloat16
U32 = mybir.dt.uint32
AF = mybir.ActivationFunctionType
ALU = mybir.AluOpType
AX = mybir.AxisListType

NCORES = 8
D = 4096
H = 11008
NHC = H // 128
NKC = D // 128
FQ = 2176
NFC = FQ // 128
EPS = 1e-5

NF4_CODE = np.array(
    [-1.0, -0.6961928009986877, -0.5250730514526367, -0.39491748809814453,
     -0.28444138169288635, -0.18477343022823334, -0.09105003625154495, 0.0,
     0.07958029955625534, 0.16093020141124725, 0.24611230194568634,
     0.33791524171829224, 0.44070982933044434, 0.5626170039176941,
     0.7229568362236023, 1.0], dtype=np.float32)
NF4_BOUNDS = ((NF4_CODE[:-1] + NF4_CODE[1:]) * np.float32(0.5)).astype(np.float32)
NF4_DELTA = (NF4_CODE[1:] - NF4_CODE[:-1]).astype(np.float32)

ENGINES = ("sync", "vector", "scalar", "tensor", "gpsimd")


def _build_consts():
    out = {}
    dd = np.arange(D, dtype=np.int64)
    ff = np.arange(FQ, dtype=np.int64)
    dmat = dd[:, None] * ff[None, :]
    ang = 2.0 * np.pi * ((dmat % D).astype(np.float64)) / D
    re = np.cos(ang)
    im = np.sin(ang)
    re[:, 2049:] = 0.0
    im[:, 2049:] = 0.0

    def _lhsT_tiles(m):
        m4 = m.reshape(NKC, 128, NFC, 128)
        return np.ascontiguousarray(
            m4.transpose(2, 1, 0, 3).reshape(NFC, 128, NKC * 128))

    out["dftre"] = _lhsT_tiles(re).astype(ml_dtypes.bfloat16)
    out["dftim"] = _lhsT_tiles(im).astype(ml_dtypes.bfloat16)
    w = np.zeros(FQ, dtype=np.float64)
    w[0] = 1.0 / D
    w[1:2048] = 2.0 / D
    w[2048] = 1.0 / D
    tt_ = np.arange(D, dtype=np.int64)
    fm = ff[:, None] * tt_[None, :]
    cosm = w[:, None] * np.cos(2.0 * np.pi * ((fm % D).astype(np.float64)) / D)
    cosm[2049:, :] = 0.0
    out["cosm"] = np.ascontiguousarray(
        cosm.reshape(NFC, 128, D)).astype(ml_dtypes.bfloat16)
    out["identb"] = np.eye(128, dtype=ml_dtypes.bfloat16)
    out["identf"] = np.eye(128, dtype=np.float32)
    out["rowbase"] = (np.arange(128, dtype=np.float32) * 8192.0).reshape(128, 1)
    return out


class KB:

    def __init__(self, nc):
        self.nc = nc
        self.steps = {e: [] for e in ENGINES}
        self.sem = {}
        self.cnt = {}
        self.ctx = ExitStack()

    def new_sem(self, *names):
        for n in names:
            self.sem[n] = self.ctx.enter_context(self.nc.semaphore(n))
            self.cnt[n] = 0

    def wait(self, eng, name, val=None):
        v = self.cnt[name] if val is None else val
        h = self.sem[name]
        if v > 0:
            self.steps[eng].append(lambda e, h=h, v=v: e.wait_ge(h, v))
        return v

    def emit(self, eng, fn, inc=None):
        pair = None
        if inc is not None:
            nm, n = inc
            self.cnt[nm] += n
            pair = (self.sem[nm], n)

        def step(e, fn=fn, pair=pair):
            ins = fn(e)
            if pair is not None:
                ins.then_inc(pair[0], pair[1])
        self.steps[eng].append(step)
        return self.cnt[inc[0]] if inc else None

    def drain(self, eng):
        self.steps[eng].append(lambda e: e.drain())

    def finalize_block(self):
        block = self.ctx.enter_context(self.nc.Block())
        for eng in ENGINES:
            steps = self.steps[eng]
            if not steps:
                continue
            dec = getattr(block, eng)

            def section(e, steps=steps):
                for s in steps:
                    s(e)
            dec(section)


def _ln_emit(B, src_ap, st, outs, sqscr=None):
    sm = st[:, 0:1]; sq = st[:, 1:2]; m = st[:, 2:3]; msq = st[:, 3:4]
    m2 = st[:, 4:5]; var = st[:, 5:6]; vpe = st[:, 6:7]; sd = st[:, 7:8]
    r0 = st[:, 8:9]; r0s = st[:, 9:10]; tm = st[:, 10:11]; co = st[:, 11:12]
    r = st[:, 12:13]; nmr = st[:, 13:14]

    B.wait("scalar", "v")
    B.drain("scalar")
    B.emit("vector", lambda e: e.reduce_sum(out=sm, in_=src_ap, axis=AX.X),
           inc=("v", 1))
    B.drain("vector")
    scr = outs[0] if sqscr is None else sqscr
    B.emit("scalar", lambda e: e.activation(out=scr, in_=src_ap,
                                            func=AF.Square, accum_out=sq),
           inc=("a", 1))
    B.drain("scalar")
    B.wait("vector", "a")
    B.emit("vector", lambda e: e.tensor_scalar_mul(m, sm, 1.0 / 4096.0))
    B.drain("vector")
    B.emit("vector", lambda e: e.tensor_scalar_mul(msq, sq, 1.0 / 4096.0))
    B.drain("vector")
    B.emit("vector", lambda e: e.tensor_mul(m2, m, m))
    B.drain("vector")
    B.emit("vector", lambda e: e.tensor_sub(var, msq, m2))
    B.drain("vector")
    B.emit("vector", lambda e: e.tensor_scalar_add(vpe, var, float(EPS)),
           inc=("v", 1))
    B.drain("vector")
    B.wait("scalar", "v")
    B.emit("scalar", lambda e: e.activation(out=sd, in_=vpe, func=AF.Sqrt),
           inc=("a", 1))
    B.drain("scalar")
    B.wait("vector", "a")
    B.emit("vector", lambda e: e.reciprocal(out=r0, in_=sd))
    B.drain("vector")
    B.emit("vector", lambda e: e.tensor_mul(r0s, r0, r0))
    B.drain("vector")
    B.emit("vector", lambda e: e.tensor_mul(tm, r0s, vpe))
    B.drain("vector")
    B.emit("vector", lambda e: e.tensor_scalar(out=co, in0=tm, scalar1=-0.5,
                                               scalar2=1.5, op0=ALU.mult,
                                               op1=ALU.add))
    B.drain("vector")
    B.emit("vector", lambda e: e.tensor_mul(r, r0, co))
    B.drain("vector")
    B.emit("vector", lambda e: e.scalar_tensor_tensor(
        out=nmr, in0=m, scalar=-1.0, in1=r, op0=ALU.mult, op1=ALU.mult),
        inc=("v", 1))
    B.drain("vector")
    B.wait("scalar", "v")
    for i, o in enumerate(outs):
        B.emit("scalar", lambda e, o=o: e.activation(
            out=o, in_=src_ap, func=AF.Identity, bias=nmr, scale=r),
            inc=("a", 1) if i == len(outs) - 1 else None)
        B.drain("scalar")
    return B.cnt["a"]


def build_nc(T=2048, debug=False, phases="012", ncores=8):
    nc = bass.Bass("TRN2", target_bir_lowering=False, debug=False,
                   num_devices=ncores)
    NTILE = T // 512

    W1T = 88 // ncores
    W3T = NKC // ncores
    x_d = nc.dram_tensor("x", [T, D], F32, kind="ExternalInput").ap()
    w1_d = nc.dram_tensor("w1s", [W1T * 128, D], F32, kind="ExternalInput").ap()
    w2_d = nc.dram_tensor("w2s", [W1T * 128, D], F32, kind="ExternalInput").ap()
    w3_d = nc.dram_tensor("w3s", [W3T * 128, H], F32, kind="ExternalInput").ap()
    dftre_d = nc.dram_tensor("dftre", [NFC, 128, NKC * 128], BF16,
                             kind="ExternalInput").ap()
    dftim_d = nc.dram_tensor("dftim", [NFC, 128, NKC * 128], BF16,
                             kind="ExternalInput").ap()
    cosm_d = nc.dram_tensor("cosm", [NFC, 128, D], BF16,
                            kind="ExternalInput").ap()
    identb_d = nc.dram_tensor("identb", [128, 128], BF16,
                              kind="ExternalInput").ap()
    identf_d = nc.dram_tensor("identf", [128, 128], F32,
                              kind="ExternalInput").ap()
    rowb_d = nc.dram_tensor("rowbase", [128, 1], F32,
                            kind="ExternalInput").ap()
    out_d = nc.dram_tensor("out", [T, D], F32, kind="ExternalOutput").ap()

    _ccsp = "Shared" if ncores > 4 else "Local"
    c1_d = nc.dram_tensor("c1T", [W1T, 128, NKC * 128], BF16,
                          kind="Internal").ap()
    c2_d = nc.dram_tensor("c2T", [W1T, 128, NKC * 128], BF16,
                          kind="Internal").ap()
    c3_d = nc.dram_tensor("c3T", [W3T, 128, NHC * 128], BF16,
                          kind="Internal").ap()
    dq1_d = nc.dram_tensor("dq1T", [ncores * W1T, 128, NKC * 128], BF16,
                           kind="Internal", addr_space=_ccsp).ap()
    dq2_d = nc.dram_tensor("dq2T", [ncores * W1T, 128, NKC * 128], BF16,
                           kind="Internal", addr_space=_ccsp).ap()
    dq3_d = nc.dram_tensor("dq3T", [ncores * W3T, 128, NHC * 128], BF16,
                           kind="Internal", addr_space=_ccsp).ap()
    x1_d = nc.dram_tensor("x1d", [T, D], F32, kind="Internal").ap()
    ln2_d = nc.dram_tensor("ln2d", [D, T], BF16, kind="Internal").ap()
    vv_d = [nc.dram_tensor(f"vvd{i}", [128, 2 * D], F32,
                           kind="Internal").ap() for i in range(4)]
    ytm_d = nc.dram_tensor("ytmd", [4, 128, D], F32, kind="Internal").ap()

    dbg = {}
    if debug:
        def dbgt(name, shape, dtype=F32):
            dbg[name] = nc.dram_tensor("dbg_" + name, shape, dtype,
                                       kind="ExternalOutput").ap()
        dbgt("ln1", [128, D]); dbgt("c", [128, D]); dbgt("top8", [128, 8])
        dbgt("idx8", [128, 8], U32); dbgt("attn", [128, D])
        dbgt("x1", [128, D]); dbgt("dq1", [128, NKC * 128])
        dbgt("yt", [128, D])

    B = KB(nc)
    B.new_sem("v", "a", "t", "cst",
              "wl_a", "wl_b", "dqs_a", "dqs_b",
              "xi_a", "xi_b", "gv", "gx", "gp", "df_a", "df_b", "cs_a", "cs_b",
              "gs_a", "gs_b", "gs_c", "gs_d", "sx1", "sl2_a", "sl2_b",
              "l2l", "w12_a", "w12_b", "w12_c", "w2_a", "w2_b", "w2_c",
              "d3_a", "d3_b", "gy_a", "gy_b",
              "ye_a", "ye_b", "so_a", "so_b", "cc", "dbgs")
    ctx = B.ctx

    identb = ctx.enter_context(nc.sbuf_tensor("identb_s", [128, 128], BF16))
    identf = ctx.enter_context(nc.sbuf_tensor("identf_s", [128, 128], F32))
    rowb = ctx.enter_context(nc.sbuf_tensor("rowb_s", [128, 1], F32))
    B.emit("sync", lambda e: e.dma_start(out=identb[:], in_=identb_d),
           inc=("cst", 16))
    B.emit("sync", lambda e: e.dma_start(out=identf[:], in_=identf_d),
           inc=("cst", 16))
    B.emit("sync", lambda e: e.dma_start(out=rowb[:], in_=rowb_d),
           inc=("cst", 16))
    const_ld = B.cnt["cst"]
    for eng in ("vector", "scalar", "tensor", "gpsimd"):
        B.wait(eng, "cst", const_ld)

    with ExitStack() as p0ctx:
        FMAX = 5504
        wt = [p0ctx.enter_context(nc.sbuf_tensor(f"wt{i}", [128, FMAX], F32))
              for i in range(2)]
        nb_t = p0ctx.enter_context(nc.sbuf_tensor("nbuf", [128, FMAX], F32))
        acc_t = p0ctx.enter_context(nc.sbuf_tensor("acc", [128, FMAX], F32))
        pl_t = p0ctx.enter_context(nc.sbuf_tensor("plane", [128, FMAX], F32))
        pl2_t = p0ctx.enter_context(
            nc.sbuf_tensor("plane2", [128, FMAX], F32))
        pl3_t = p0ctx.enter_context(
            nc.sbuf_tensor("plane3", [128, FMAX], F32))
        am_t = p0ctx.enter_context(nc.sbuf_tensor("am", [128, 92], F32))
        eq_t = p0ctx.enter_context(nc.sbuf_tensor("eqz", [128, 92], F32))
        ag_t = p0ctx.enter_context(nc.sbuf_tensor("amg", [128, 92], F32))
        dq_t = [p0ctx.enter_context(
            nc.sbuf_tensor(f"dqb{i}", [128, FMAX], BF16)) for i in range(2)]
        stg = [p0ctx.enter_context(nc.sbuf_tensor(f"stg{i}", [128, 1024], BF16))
               for i in range(2)]
        ptr = [p0ctx.enter_context(nc.psum_tensor(f"ptr{i}", [128, 1024], BF16))
               for i in range(2)]

        jobs = []
        for hc in range(W1T):
            jobs.append((w1_d[hc * 128:(hc + 1) * 128, :], D, c1_d, hc, 0))
        for hc in range(W1T):
            jobs.append((w2_d[hc * 128:(hc + 1) * 128, :], D, c2_d, hc, 0))
        for dc in range(W3T):
            for half in range(2):
                jobs.append((w3_d[dc * 128:(dc + 1) * 128,
                                  half * 5504:(half + 1) * 5504], 5504,
                             c3_d, dc, half * 43))

        wl_sems = ["wl_a", "wl_b"]
        dqs_sems = ["dqs_a", "dqs_b"]
        v_after_div = {}
        v_comb = {}
        v_after_evac = {}
        t_tile_last = {}
        gidx = 0

        for ji, (src_ap, F, dst_ap, c0, h0) in enumerate(jobs):
            slot = ji % 2
            if ji >= 2:
                B.wait("sync", "v", v_after_div[ji - 2])
            B.emit("sync",
                   lambda e, src_ap=src_ap, F=F, slot=slot:
                   e.dma_start(out=wt[slot][:, 0:F], in_=src_ap),
                   inc=(wl_sems[slot], 16))
            wl_tot = B.cnt[wl_sems[slot]]
            nbk = F // 64
            wv = wt[slot][:, 0:F].rearrange("p (nb e) -> p nb e", e=64)
            amv = am_t[:, 0:nbk]
            amo = am_t[:, 0:nbk].rearrange("p (nb one) -> p nb one", one=1)
            B.wait("vector", wl_sems[slot], wl_tot)
            B.emit("vector", lambda e, wv=wv, amo=amo: e.tensor_reduce(
                out=amo, in_=wv, axis=AX.X, op=ALU.max,
                apply_absolute_value=True))
            B.drain("vector")
            B.emit("vector", lambda e, amv=amv, nbk=nbk: e.tensor_scalar(
                out=eq_t[:, 0:nbk], in0=amv, scalar1=0.0, scalar2=None,
                op0=ALU.is_equal))
            B.drain("vector")
            B.emit("vector", lambda e, amv=amv, nbk=nbk: e.tensor_add(
                ag_t[:, 0:nbk], amv, eq_t[:, 0:nbk]))
            B.drain("vector")
            B.emit("vector", lambda e, nbk=nbk: e.reciprocal(
                out=eq_t[:, 0:nbk], in_=ag_t[:, 0:nbk]))
            B.drain("vector")
            rgb = eq_t[:, 0:nbk].unsqueeze(2).broadcast_to([128, nbk, 64])
            agb = ag_t[:, 0:nbk].unsqueeze(2).broadcast_to([128, nbk, 64])
            nv = nb_t[:, 0:F].rearrange("p (nb e) -> p nb e", e=64)
            B.emit("vector", lambda e, wv=wv, rgb=rgb, nv=nv: e.tensor_tensor(
                out=nv, in0=wv, in1=rgb, op=ALU.mult), inc=("v", 1))
            v_after_div[ji] = B.cnt["v"]
            B.drain("vector")
            nfl = nb_t[:, 0:F]
            B.emit("vector", lambda e, nfl=nfl, F=F: e.tensor_scalar(
                out=acc_t[:, 0:F], in0=nfl,
                scalar1=float(NF4_BOUNDS[0]), scalar2=float(NF4_DELTA[0]),
                op0=ALU.is_gt, op1=ALU.mult))
            B.emit("vector", lambda e, nfl=nfl, F=F: e.tensor_scalar(
                out=pl2_t[:, 0:F], in0=nfl,
                scalar1=float(NF4_BOUNDS[1]), scalar2=float(NF4_DELTA[1]),
                op0=ALU.is_gt, op1=ALU.mult))
            B.drain("vector")
            for b in range(2, 15, 2):
                B.emit("vector", lambda e, nfl=nfl, F=F, b=b: e.tensor_scalar(
                    out=pl_t[:, 0:F], in0=nfl,
                    scalar1=float(NF4_BOUNDS[b]), scalar2=float(NF4_DELTA[b]),
                    op0=ALU.is_gt, op1=ALU.mult))
                if b + 1 < 15:
                    B.emit("vector", lambda e, nfl=nfl, F=F, b=b:
                           e.tensor_scalar(
                               out=pl3_t[:, 0:F], in0=nfl,
                               scalar1=float(NF4_BOUNDS[b + 1]),
                               scalar2=float(NF4_DELTA[b + 1]),
                               op0=ALU.is_gt, op1=ALU.mult))
                B.drain("vector")
                B.emit("vector", lambda e, F=F: e.tensor_add(
                    acc_t[:, 0:F], acc_t[:, 0:F], pl_t[:, 0:F]))
                if b + 1 < 15:
                    B.emit("vector", lambda e, F=F: e.tensor_add(
                        pl2_t[:, 0:F], pl2_t[:, 0:F], pl3_t[:, 0:F]))
                B.drain("vector")
            B.emit("vector", lambda e, F=F: e.tensor_add(
                acc_t[:, 0:F], acc_t[:, 0:F], pl2_t[:, 0:F]))
            B.drain("vector")
            if ji >= 2:
                B.wait("vector", "t", t_tile_last[ji - 2])
            dqv = dq_t[slot][:, 0:F].rearrange("p (nb e) -> p nb e", e=64)
            accv = acc_t[:, 0:F].rearrange("p (nb e) -> p nb e", e=64)
            B.emit("vector", lambda e, accv=accv, agb=agb, dqv=dqv:
                   e.scalar_tensor_tensor(out=dqv, in0=accv, scalar=-1.0,
                                          in1=agb, op0=ALU.add, op1=ALU.mult),
                   inc=("v", 1))
            v_dq = B.cnt["v"]
            B.drain("vector")
            ngroups = [(g * 8, 8) for g in range(F // 1024)]
            if F % 1024:
                ngroups.append(((F // 1024) * 8, (F % 1024) // 128))
            B.wait("tensor", "v", v_dq)
            for (k0, ng) in ngroups:
                par = gidx % 2
                if gidx >= 2:
                    B.wait("tensor", "v", v_after_evac[gidx - 2])
                for j in range(ng):
                    B.emit("tensor",
                           lambda e, k0=k0, j=j, par=par, slot=slot:
                           e.transpose(
                               out=ptr[par][:, j * 128:(j + 1) * 128],
                               in_=dq_t[slot][:, (k0 + j) * 128:
                                              (k0 + j + 1) * 128],
                               identity=identb[:]),
                           inc=("t", 1) if j == ng - 1 else None)
                t_grp = B.cnt["t"]
                ssl = gidx % 2
                B.wait("vector", "t", t_grp)
                B.wait("vector", dqs_sems[ssl], B.cnt[dqs_sems[ssl]])
                B.emit("vector", lambda e, ng=ng, par=par, ssl=ssl:
                       e.tensor_copy(out=stg[ssl][:, 0:ng * 128],
                                     in_=ptr[par][:, 0:ng * 128]),
                       inc=("v", 1))
                B.drain("vector")
                v_after_evac[gidx] = B.cnt["v"]
                dst = dst_ap[c0].rearrange("p (kk h) -> p kk h", h=128)[
                    :, h0 + k0:h0 + k0 + ng, :]
                B.wait("sync", "v", v_after_evac[gidx])
                B.emit("sync", lambda e, dst=dst, ng=ng, ssl=ssl: e.dma_start(
                    out=dst,
                    in_=stg[ssl][:, 0:ng * 128].rearrange(
                        "p (kk h) -> p kk h", h=128)),
                    inc=(dqs_sems[ssl], 16))
                gidx += 1
            t_tile_last[ji] = B.cnt["t"]
            if debug and ji == 0:
                B.wait("sync", "v", v_dq)
                B.emit("sync", lambda e: e.dma_start(out=dbg["dq1"],
                                                     in_=dq_t[0][:, 0:D]),
                       inc=("dbgs", 16))

        p0_dqs_a = B.cnt["dqs_a"]
        p0_dqs_b = B.cnt["dqs_b"]

    B.wait("gpsimd", "dqs_a", p0_dqs_a)
    B.wait("gpsimd", "dqs_b", p0_dqs_b)
    grp = [list(range(ncores))]
    for (cin, cout) in ((c1_d, dq1_d), (c2_d, dq2_d), (c3_d, dq3_d)):
        B.emit("gpsimd", lambda e, cin=cin, cout=cout: e.collective_compute(
            "AllGather", ALU.bypass, replica_groups=grp,
            ins=[cin.rearrange("t p c -> (t p) c")],
            outs=[cout.rearrange("t p c -> (t p) c")]),
            inc=("cc", 1))
    p0_cc = B.cnt["cc"]

    if "1" in phases:
        for eng in ENGINES:
            B.wait(eng, "dqs_a", p0_dqs_a)
            B.wait(eng, "dqs_b", p0_dqs_b)
            if debug:
                B.wait(eng, "dbgs")

        with ExitStack() as p1ctx:
            xb = p1ctx.enter_context(nc.sbuf_tensor("xb", [128, D], F32))
            castC = [p1ctx.enter_context(
                nc.sbuf_tensor(f"castC{i}", [128, D], BF16))
                for i in range(2)]
            ln1fm = p1ctx.enter_context(
                nc.sbuf_tensor("ln1fm", [128, NKC * 512], BF16))
            st_t = p1ctx.enter_context(nc.sbuf_tensor("lnst", [128, 16], F32))
            pfm = p1ctx.enter_context(
                nc.sbuf_tensor("pfm", [128, NFC * 512], BF16))
            l2st0 = p1ctx.enter_context(
                nc.sbuf_tensor("l2st0", [128, 1024], BF16))
            l2st = [l2st0, l2st0]
            top8 = [p1ctx.enter_context(nc.sbuf_tensor(
                f"top8_{i}", [128, 8], F32)) for i in range(2)]
            idx8u = [p1ctx.enter_context(nc.sbuf_tensor(
                f"idx8u_{i}", [128, 8], U32)) for i in range(2)]
            z8 = [p1ctx.enter_context(nc.sbuf_tensor(
                f"z8_{i}", [128, 8], F32)) for i in range(2)]
            e8 = [p1ctx.enter_context(nc.sbuf_tensor(
                f"e8_{i}", [128, 8], F32)) for i in range(2)]
            w8 = [p1ctx.enter_context(nc.sbuf_tensor(
                f"w8_{i}", [128, 8], F32)) for i in range(2)]
            d32 = [p1ctx.enter_context(nc.sbuf_tensor(
                f"d32_{i}", [128, 8], F32)) for i in range(2)]
            fidx = [p1ctx.enter_context(nc.sbuf_tensor(
                f"fidx_{i}", [128, 8], F32)) for i in range(2)]
            gidxu = [p1ctx.enter_context(nc.sbuf_tensor(
                f"gidxu_{i}", [128, 8], U32)) for i in range(2)]
            dfr = [p1ctx.enter_context(
                nc.sbuf_tensor(f"dfr{i}", [128, 2 * NKC * 128], BF16))
                for i in range(2)]
            sqa = p1ctx.enter_context(nc.sbuf_tensor("sqa", [128, 512], BF16))
            sqb = p1ctx.enter_context(nc.sbuf_tensor("sqb", [128, 512], BF16))
            cosb = [p1ctx.enter_context(
                nc.sbuf_tensor(f"cosb{i}", [128, 2048], BF16))
                for i in range(2)]
            cx = p1ctx.enter_context(nc.sbuf_tensor("cx", [128, D], F32))
            shb = [p1ctx.enter_context(
                nc.sbuf_tensor(f"shb{i}", [128, D], BF16)) for i in range(4)]
            acb = [p1ctx.enter_context(
                nc.sbuf_tensor(f"acb{i}", [128, D], F32)) for i in range(2)]
            cps4 = [p1ctx.enter_context(
                nc.psum_tensor(f"cps{i}", [128, 512], F32)) for i in range(4)]
            breB = p1ctx.enter_context(nc.psum_tensor("breB", [128, 512], F32))
            bimB = p1ctx.enter_context(nc.psum_tensor("bimB", [128, 512], F32))
            ptrP = [p1ctx.enter_context(
                nc.psum_tensor(f"ptrP{i}", [128, 1024], BF16))
                for i in range(2)]

            df_sems = ["df_a", "df_b"]
            cs_sems = ["cs_a", "cs_b"]
            gs_sems = ["gs_a", "gs_b", "gs_c", "gs_d"]
            sl2_sems = ["sl2_a", "sl2_a"]
            gtile = 0
            ev_p1 = {}
            xb_credit = [None]
            v_acc7 = {}
            a_ln2_S = {}

            def emit_transposes(src_bf, dst_fn, to_dram):
                nonlocal gtile
                for g in range(4):
                    par = gtile % 2
                    if gtile >= 2:
                        B.wait("tensor", "v", ev_p1[gtile - 2])
                    for j in range(8):
                        B.emit("tensor", lambda e, g=g, j=j, par=par:
                               e.transpose(
                                   out=ptrP[par][:, j * 128:(j + 1) * 128],
                                   in_=src_bf[:, (g * 8 + j) * 128:
                                              (g * 8 + j + 1) * 128],
                                   identity=identb[:]),
                               inc=("t", 1) if j == 7 else None)
                    t_tot = B.cnt["t"]
                    B.wait("vector", "t", t_tot)
                    if to_dram:
                        B.wait("vector", "sl2_a", B.cnt["sl2_a"])
                        B.emit("vector", lambda e, par=par:
                               e.tensor_copy(out=l2st0[:],
                                             in_=ptrP[par][:]),
                               inc=("v", 1))
                        B.drain("vector")
                        ev_p1[gtile] = B.cnt["v"]
                        dst = dst_fn(g)
                        B.wait("sync", "v", ev_p1[gtile])
                        B.emit("sync", lambda e, dst=dst:
                               e.dma_start(
                                   out=dst, in_=l2st0[:].rearrange(
                                       "p (kk h) -> p kk h", h=128)),
                               inc=("sl2_a", 16))
                    else:
                        dst = dst_fn(g)
                        B.emit("vector", lambda e, par=par, dst=dst:
                               e.tensor_copy(
                                   out=dst, in_=ptrP[par][:].rearrange(
                                       "p (kk h) -> p kk h", h=128)),
                               inc=("v", 1))
                        B.drain("vector")
                        ev_p1[gtile] = B.cnt["v"]
                    gtile += 1

            def _p1_tile(tt):
                for s in range(4):
                    S = tt * 4 + s
                    if xb_credit[0] is not None:
                        B.wait("sync", *xb_credit[0])
                    B.emit("sync", lambda e, tt=tt, s=s:
                           e.dma_start(
                               out=xb[:],
                               in_=x_d[tt * 512 + s * 128:
                                       tt * 512 + (s + 1) * 128, :]),
                           inc=("xi_a", 16))
                    xi_tot = B.cnt["xi_a"]
                    B.wait("vector", "xi_a", xi_tot)
                    B.wait("scalar", "xi_a", xi_tot)
                    B.wait("scalar", "t")
                    B.wait("scalar", "gv", B.cnt["gv"])
                    a_norm = _ln_emit(B, xb[:], st_t[:], [castC[s % 2][:]])
                    xb_credit[0] = ("a", a_norm)
                    B.wait("gpsimd", "a", a_norm)
                    for nm in gs_sems:
                        B.wait("gpsimd", nm, B.cnt[nm])
                    B.emit("gpsimd", lambda e, s=s: e.dma_start(
                        out=vv_d[s][:, 0:D], in_=castC[s % 2][:]),
                        inc=("gv", 16))
                    B.emit("gpsimd", lambda e, s=s: e.dma_start(
                        out=vv_d[s][:, D:2 * D], in_=castC[s % 2][:]),
                        inc=("gv", 16))
                    if debug and tt == 0 and s == 0:
                        B.wait("sync", "a", a_norm)
                        B.emit("sync", lambda e, s=s: e.dma_start(
                            out=dbg["ln1"], in_=castC[s % 2][:]),
                            inc=("dbgs", 16))
                    B.wait("tensor", "a", a_norm)

                    def _dst_ln1(g, s=s):
                        return ln1fm[:].rearrange(
                            "p (kc t) -> p kc t", t=512)[
                            :, g * 8:(g + 1) * 8, s * 128:(s + 1) * 128]
                    emit_transposes(castC[s % 2], _dst_ln1, False)
                v_ln1fm_done = B.cnt["v"]

                B.wait("tensor", "v", v_ln1fm_done)
                B.wait("tensor", "a")
                B.wait("scalar", "v")
                B.wait("scalar", "sx1")
                v_p_done = {}
                t_dft_fc = {}
                a_sq_fc = {}
                for fc in range(NFC):
                    par = fc % 2
                    if fc >= 2:
                        B.wait("scalar", "t", t_dft_fc[fc - 2])
                    B.emit("scalar", lambda e, fc=fc, par=par: e.dma_start(
                        out=dfr[par][:, 0:NKC * 128], in_=dftre_d[fc]),
                        inc=(df_sems[par], 16))
                    B.emit("scalar", lambda e, fc=fc, par=par: e.dma_start(
                        out=dfr[par][:, NKC * 128:2 * NKC * 128],
                        in_=dftim_d[fc]),
                        inc=(df_sems[par], 16))
                    df_tot = B.cnt[df_sems[par]]
                    B.wait("tensor", df_sems[par], df_tot)
                    if fc >= 1:
                        B.wait("tensor", "a", a_sq_fc[fc - 1])
                    for kc in range(NKC):
                        B.emit("tensor", lambda e, kc=kc, par=par: e.matmul(
                            breB[:],
                            lhsT=dfr[par][:, kc * 128:(kc + 1) * 128],
                            rhs=ln1fm[:, kc * 512:(kc + 1) * 512],
                            start=(kc == 0), stop=(kc == NKC - 1)))
                    for kc in range(NKC):
                        B.emit("tensor", lambda e, kc=kc, par=par: e.matmul(
                            bimB[:],
                            lhsT=dfr[par][:, NKC * 128 + kc * 128:
                                          NKC * 128 + (kc + 1) * 128],
                            rhs=ln1fm[:, kc * 512:(kc + 1) * 512],
                            start=(kc == 0), stop=(kc == NKC - 1)),
                            inc=("t", 1) if kc == NKC - 1 else None)
                    t_dft_fc[fc] = B.cnt["t"]
                    B.wait("scalar", "t", t_dft_fc[fc])
                    if fc >= 1:
                        B.wait("scalar", "v", v_p_done[fc - 1])
                    B.drain("scalar")
                    B.emit("scalar", lambda e: e.activation(
                        out=sqa[:], in_=breB[:], func=AF.Square))
                    B.drain("scalar")
                    B.emit("scalar", lambda e: e.activation(
                        out=sqb[:], in_=bimB[:], func=AF.Square),
                        inc=("a", 1))
                    B.drain("scalar")
                    a_sq_fc[fc] = B.cnt["a"]
                    B.wait("vector", "a", a_sq_fc[fc])
                    B.emit("vector", lambda e, fc=fc: e.tensor_add(
                        pfm[:, fc * 512:(fc + 1) * 512], sqa[:], sqb[:]),
                        inc=("v", 1))
                    B.drain("vector")
                    v_p_done[fc] = B.cnt["v"]

                def _part_a(s, cgate):
                    S = tt * 4 + s
                    ss = s % 2
                    v_evac = None
                    for half in range(2):
                        ntc = 4 if half == 0 else 1
                        cs_t = {}
                        for fc in range(NFC):
                            par = fc % 2
                            if fc >= 2:
                                B.wait("scalar", "t", cs_t[fc - 2])
                            else:
                                B.wait("scalar", "t")
                            B.emit("scalar", lambda e, fc=fc, half=half,
                                   par=par, ntc=ntc: e.dma_start(
                                       out=cosb[par][:, 0:ntc * 512],
                                       in_=cosm_d[fc][:, half * 2048:
                                                      half * 2048 +
                                                      ntc * 512]),
                                   inc=(cs_sems[par], 16))
                            cs_tot = B.cnt[cs_sems[par]]
                            B.wait("tensor", cs_sems[par], cs_tot)
                            if fc == 0:
                                if half == 0 and cgate is None:
                                    B.wait("tensor", "a")
                                    B.wait("tensor", "v")
                                elif half == 0:
                                    B.wait("tensor", "v",
                                           max(cgate,
                                               v_p_done[NFC - 1]))
                                else:
                                    B.wait("tensor", "v", v_evac)
                            for tc in range(ntc):
                                B.emit("tensor", lambda e, fc=fc, s=s,
                                       tc=tc, par=par: e.matmul(
                                           cps4[tc][:],
                                           lhsT=pfm[:, fc * 512 + s * 128:
                                                    fc * 512 +
                                                    (s + 1) * 128],
                                           rhs=cosb[par][:, tc * 512:
                                                         (tc + 1) * 512],
                                           start=(fc == 0),
                                           stop=(fc == NFC - 1)),
                                       inc=("t", 1) if tc == ntc - 1
                                       else None)
                            cs_t[fc] = B.cnt["t"]
                        t_c = B.cnt["t"]
                        B.wait("vector", "t", t_c)
                        for tc in range(ntc):
                            B.emit("vector", lambda e, tc=tc, half=half:
                                   e.tensor_copy(
                                       out=cx[:, half * 2048 + tc * 512:
                                              half * 2048 + (tc + 1) * 512],
                                       in_=cps4[tc][:]),
                                   inc=("v", 1) if tc == ntc - 1 else None)
                            B.drain("vector")
                        v_evac = B.cnt["v"]
                    B.emit("vector", lambda e: e.tensor_copy(
                        out=cx[:, 2560:4096],
                        in_=bass.AP(cx, 1536, [[4096, 128], [-1, 1536]])),
                        inc=("v", 1))
                    B.drain("vector")
                    v_c = B.cnt["v"]
                    if debug and tt == 0 and s == 0:
                        B.wait("sync", "v", v_c)
                        B.emit("sync", lambda e: e.dma_start(out=dbg["c"],
                                                             in_=cx[:]),
                               inc=("dbgs", 16))
                    B.emit("vector", lambda e, ss=ss: e.max(
                        out=top8[ss][:], in_=cx[:]))
                    B.drain("vector")
                    B.emit("vector", lambda e, ss=ss: e.max_index(
                        out=idx8u[ss][:], in_max=top8[ss][:],
                        in_values=cx[:]))
                    B.drain("vector")
                    B.emit("vector", lambda e, ss=ss: e.tensor_scalar(
                        out=z8[ss][:], in0=top8[ss][:],
                        scalar1=top8[ss][:, 0:1],
                        scalar2=-80.0, op0=ALU.subtract, op1=ALU.max),
                        inc=("v", 1))
                    B.drain("vector")
                    v_z = B.cnt["v"]
                    B.wait("scalar", "v", v_z)
                    B.drain("scalar")
                    B.emit("scalar", lambda e, ss=ss: e.activation(
                        out=e8[ss][:], in_=z8[ss][:], func=AF.Exp),
                        inc=("a", 1))
                    B.drain("scalar")
                    a_e = B.cnt["a"]
                    B.wait("vector", "a", a_e)
                    B.emit("vector", lambda e, ss=ss: e.reduce_sum(
                        out=st_t[:, 14:15], in_=e8[ss][:], axis=AX.X))
                    B.drain("vector")
                    B.emit("vector", lambda e: e.reciprocal(
                        out=st_t[:, 15:16], in_=st_t[:, 14:15]))
                    B.drain("vector")
                    B.emit("vector", lambda e, ss=ss: e.tensor_scalar(
                        out=w8[ss][:], in0=e8[ss][:],
                        scalar1=st_t[:, 15:16],
                        scalar2=None, op0=ALU.mult))
                    B.drain("vector")
                    B.emit("vector", lambda e, ss=ss: e.tensor_copy(
                        out=d32[ss][:], in_=idx8u[ss][:]))
                    B.drain("vector")
                    B.emit("vector", lambda e, ss=ss: e.tensor_add(
                        fidx[ss][:], d32[ss][:],
                        rowb[:].broadcast_to([128, 8])))
                    B.drain("vector")
                    B.emit("vector", lambda e, ss=ss: e.tensor_copy(
                        out=gidxu[ss][:], in_=fidx[ss][:]),
                        inc=("v", 1))
                    B.drain("vector")
                    v_idx = B.cnt["v"]
                    if debug and tt == 0 and s == 0:
                        B.wait("sync", "v", v_idx)
                        B.emit("sync", lambda e, ss=ss: e.dma_start(
                            out=dbg["top8"], in_=top8[ss][:]),
                            inc=("dbgs", 16))
                        B.emit("sync", lambda e, ss=ss: e.dma_start(
                            out=dbg["idx8"], in_=idx8u[ss][:]),
                            inc=("dbgs", 16))
                    vv_flat = vv_d[s].rearrange("p f -> (p f)").unsqueeze(1)
                    B.wait("gpsimd", "v", v_idx)
                    B.wait("gpsimd", "gv", B.cnt["gv"])
                    if S >= 2:
                        B.wait("gpsimd", "v", v_acc7[S - 2])
                    gs_tots = []
                    for k in range(2):
                        gp = 2 * ss + (k % 2)
                        B.emit("gpsimd",
                               lambda e, k=k, gp=gp, ss=ss,
                               vv_flat=vv_flat:
                               e.indirect_dma_start(
                                   out=shb[gp][:], out_offset=None,
                                   in_=vv_flat,
                                   in_offset=IndirectOffsetOnAxis(
                                       gidxu[ss][:, k:k + 1], 0)),
                               inc=(gs_sems[gp], 16))
                        gs_tots.append(B.cnt[gs_sems[gp]])
                    return v_c, (gs_tots, vv_flat)

                def _part_b(s, ga):
                    gs_tots, vv_flat = ga
                    S = tt * 4 + s
                    ss = s % 2
                    if S >= 1 and (S - 1) in a_ln2_S:
                        B.wait("vector", "a", a_ln2_S[S - 1])
                    B.wait("vector", "sx1")
                    v_acc = {}
                    for k in range(8):
                        gp = 2 * ss + (k % 2)
                        if k >= 2:
                            B.wait("gpsimd", "v", v_acc[k - 2])
                            B.emit("gpsimd",
                                   lambda e, k=k, gp=gp, ss=ss,
                                   vv_flat=vv_flat:
                                   e.indirect_dma_start(
                                       out=shb[gp][:], out_offset=None,
                                       in_=vv_flat,
                                       in_offset=IndirectOffsetOnAxis(
                                           gidxu[ss][:, k:k + 1], 0)),
                                   inc=(gs_sems[gp], 16))
                            gs_tots.append(B.cnt[gs_sems[gp]])
                        B.wait("vector", gs_sems[gp], gs_tots[k])
                        if k == 0:
                            B.emit("vector", lambda e, gp=gp, ss=ss:
                                   e.tensor_scalar(
                                       out=acb[0][:], in0=shb[gp][:],
                                       scalar1=w8[ss][:, 0:1], scalar2=None,
                                       op0=ALU.mult), inc=("v", 1))
                        else:
                            B.emit("vector", lambda e, k=k, gp=gp, ss=ss:
                                   e.scalar_tensor_tensor(
                                       out=acb[k % 2][:], in0=shb[gp][:],
                                       scalar=w8[ss][:, k:k + 1],
                                       in1=acb[(k - 1) % 2][:],
                                       op0=ALU.mult, op1=ALU.add),
                                   inc=("v", 1))
                        B.drain("vector")
                        v_acc[k] = B.cnt["v"]
                    v_acc7[S] = v_acc[7]
                    attn_ap = acb[1][:]
                    if debug and tt == 0 and s == 0:
                        B.wait("sync", "v", v_acc[7])
                        B.emit("sync", lambda e: e.dma_start(
                            out=dbg["attn"], in_=attn_ap), inc=("dbgs", 16))
                    B.wait("vector", "v", v_acc[7])
                    B.wait("scalar", "v", v_acc[7])
                    _ln_emit(B, attn_ap, st_t[:], [acb[0][:]])
                    a_lnat = B.cnt["a"]
                    B.wait("sync", "a", a_lnat)
                    B.emit("sync", lambda e, tt=tt, s=s: e.dma_start(
                        out=acb[1][:],
                        in_=x_d[tt * 512 + s * 128:
                                tt * 512 + (s + 1) * 128, :]),
                        inc=("xi_b", 16))
                    xr_tot = B.cnt["xi_b"]
                    B.wait("vector", "a", a_lnat)
                    B.wait("vector", "xi_b", xr_tot)
                    B.emit("vector", lambda e: e.tensor_add(
                        acb[0][:], acb[0][:], acb[1][:]), inc=("v", 1))
                    B.drain("vector")
                    v_x1 = B.cnt["v"]
                    if debug and tt == 0 and s == 0:
                        B.wait("sync", "v", v_x1)
                        B.emit("sync", lambda e: e.dma_start(
                            out=dbg["x1"], in_=acb[0][:]), inc=("dbgs", 16))
                    B.wait("sync", "v", v_x1)
                    B.emit("sync", lambda e, tt=tt, s=s: e.dma_start(
                        out=x1_d[tt * 512 + s * 128:
                                 tt * 512 + (s + 1) * 128, :],
                        in_=acb[0][:]), inc=("sx1", 16))
                    B.wait("scalar", "v", v_x1)
                    B.wait("scalar", "t")
                    B.wait("scalar", "gv", B.cnt["gv"])
                    _ln_emit(B, acb[0][:], st_t[:], [castC[s % 2][:]])
                    a_ln2_S[S] = B.cnt["a"]
                    return a_ln2_S[S]

                def _tr(s, a_ln2):
                    B.wait("tensor", "a", a_ln2)

                    def _dst_ln2(g, tt=tt, s=s):
                        return ln2_d.rearrange(
                            "(kc p) t -> p kc t", p=128)[
                            :, g * 8:(g + 1) * 8,
                            tt * 512 + s * 128: tt * 512 + (s + 1) * 128]
                    emit_transposes(castC[s % 2], _dst_ln2, True)

                return _part_a, _part_b, _tr

            _cg = None
            for _tt in range(NTILE):
                _pa, _pb, _ptr = _p1_tile(_tt)
                _ga = {}
                _cg, _ga[0] = _pa(0, _cg)
                _cg, _ga[1] = _pa(1, _cg)
                _a2 = _pb(0, _ga[0])
                _cg, _ga[2] = _pa(2, _cg)
                _a2b = _pb(1, _ga[1])
                _ptr(0, _a2)
                _cg, _ga[3] = _pa(3, _cg)
                _a2c = _pb(2, _ga[2])
                _ptr(1, _a2b)
                _a2d = _pb(3, _ga[3])
                _ptr(2, _a2c)
                _ptr(3, _a2d)

            p1_sx1 = B.cnt["sx1"]
            p1_sl2a = B.cnt["sl2_a"]
            p1_sl2b = B.cnt["sl2_b"]

    for eng in ENGINES:
        B.wait(eng, "sx1", B.cnt["sx1"])
        B.wait(eng, "sl2_a", B.cnt["sl2_a"])
        B.wait(eng, "sl2_b", B.cnt["sl2_b"])
        B.wait(eng, "gv")
        B.wait(eng, "gx")
        B.wait(eng, "gs_a")
        B.wait(eng, "gs_b")
        B.wait(eng, "gs_c")
        B.wait(eng, "gs_d")
        if debug:
            B.wait(eng, "dbgs")

    if "2" in phases:
        for eng in ENGINES:
            B.wait(eng, "sx1", B.cnt["sx1"])
            B.wait(eng, "sl2_a", B.cnt["sl2_a"])
            B.wait(eng, "sl2_b", B.cnt["sl2_b"])
            B.wait(eng, "gv")
            B.wait(eng, "gs_a")
            B.wait(eng, "gs_b")
            if debug:
                B.wait(eng, "dbgs")

        with ExitStack() as p2ctx:
            actb = p2ctx.enter_context(
                nc.sbuf_tensor("actb", [128, NHC * 512], BF16))
            st2 = p2ctx.enter_context(nc.sbuf_tensor("st2", [128, 16], F32))
            ytsb = p2ctx.enter_context(nc.sbuf_tensor("ytsb", [128, D], F32))
            outb = [p2ctx.enter_context(nc.sbuf_tensor(
                f"outb{i}", [128, D], F32)) for i in range(2)]
            pg = [p2ctx.enter_context(
                nc.psum_tensor(f"pg{i}", [128, 512], F32)) for i in range(2)]
            pu = [p2ctx.enter_context(
                nc.psum_tensor(f"pu{i}", [128, 512], F32)) for i in range(2)]
            pyf = [p2ctx.enter_context(
                nc.psum_tensor(f"pyf{i}", [128, 512], F32)) for i in range(2)]
            pyt = [p2ctx.enter_context(
                nc.psum_tensor(f"pyt{i}", [128, 512], F32)) for i in range(2)]

            w12_sems = ["w12_a", "w12_b"]
            d3_sems = ["d3_a", "d3_b"]
            gy_sems = ["gy_a", "gy_b"]
            so_sems = ["so_a", "so_b"]

            B.wait("sync", "dqs_a", p0_dqs_a)
            B.wait("sync", "dqs_b", p0_dqs_b)
            B.wait("sync", "sl2_a", B.cnt["sl2_a"])
            B.wait("sync", "sl2_b", B.cnt["sl2_b"])
            B.wait("sync", "sx1", B.cnt["sx1"])
            B.wait("sync", "cc", p0_cc)
            B.wait("scalar", "cc", p0_cc)
            B.wait("scalar", "dqs_a", p0_dqs_a)
            B.wait("scalar", "dqs_b", p0_dqs_b)

            def _p2_tile(tt):
                with ExitStack() as m12ctx:
                    l2s = m12ctx.enter_context(nc.sbuf_tensor(
                        f"l2s{tt}", [128, NKC * 512], BF16))
                    w1s = [m12ctx.enter_context(nc.sbuf_tensor(
                        f"w1s{tt}_{i}", [128, NKC * 128], BF16))
                        for i in range(2)]
                    w2s = [m12ctx.enter_context(nc.sbuf_tensor(
                        f"w2s{tt}_{i}", [128, NKC * 128], BF16))
                        for i in range(2)]
                    sgs = [m12ctx.enter_context(nc.sbuf_tensor(
                        f"sgs{tt}_{i}", [128, 512], F32)) for i in range(2)]

                    l2dst = l2s[:].rearrange("p (kc t) -> p kc t", t=512)
                    l2src = ln2_d.rearrange("(kc p) t -> p kc t", p=128)[
                        :, :, tt * 512:(tt + 1) * 512]
                    B.wait("sync", "t")
                    B.wait("sync", "gy_a", B.cnt["gy_a"])
                    B.wait("sync", "gy_b", B.cnt["gy_b"])
                    B.wait("vector", "a")
                    B.emit("sync", lambda e, l2dst=l2dst, l2src=l2src:
                           e.dma_start(out=l2dst, in_=l2src), inc=("l2l", 16))
                    l2_tot = B.cnt["l2l"]
                    B.wait("tensor", "l2l", l2_tot)
                    B.wait("tensor", "v")

                    t_mm = {}
                    v_act = {}
                    w1_sems = ["w12_a", "w12_b"]
                    w2_sems = ["w2_a", "w2_b"]
                    for hc in range(NHC):
                        par = hc % 2
                        sl3 = hc % 2
                        if hc >= 2:
                            B.wait("sync", "t", t_mm[hc - 2])
                        B.emit("sync", lambda e, hc=hc, sl3=sl3: e.dma_start(
                            out=w1s[sl3][:], in_=dq1_d[hc]),
                            inc=(w1_sems[sl3], 16))
                        B.emit("sync", lambda e, hc=hc, par=par:
                               e.dma_start(
                                   out=w2s[par][:], in_=dq2_d[hc]),
                               inc=(w2_sems[par], 16))
                        w1_tot = B.cnt[w1_sems[sl3]]
                        w2_tot = B.cnt[w2_sems[par]]
                        B.wait("tensor", w1_sems[sl3], w1_tot)
                        B.wait("tensor", w2_sems[par], w2_tot)
                        if hc >= 2:
                            B.wait("tensor", "v", v_act[hc - 2])
                        for kc in range(NKC):
                            B.emit("tensor", lambda e, kc=kc, par=par,
                                   sl3=sl3: e.matmul(
                                pg[par][:],
                                lhsT=w1s[sl3][:, kc * 128:(kc + 1) * 128],
                                rhs=l2s[:, kc * 512:(kc + 1) * 512],
                                start=(kc == 0), stop=(kc == NKC - 1)))
                        for kc in range(NKC):
                            B.emit("tensor", lambda e, kc=kc, par=par,
                                   sl3=sl3: e.matmul(
                                pu[par][:],
                                lhsT=w2s[par][:, kc * 128:(kc + 1) * 128],
                                rhs=l2s[:, kc * 512:(kc + 1) * 512],
                                start=(kc == 0), stop=(kc == NKC - 1)),
                                inc=("t", 1) if kc == NKC - 1 else None)
                        t_mm[hc] = B.cnt["t"]
                        B.wait("scalar", "t", t_mm[hc])
                        if hc >= 2:
                            B.wait("scalar", "v", v_act[hc - 2])
                        B.drain("scalar")
                        B.emit("scalar", lambda e, par=par: e.activation(
                            out=sgs[par][:], in_=pg[par][:], func=AF.Silu),
                            inc=("a", 1))
                        B.drain("scalar")
                        a_si = B.cnt["a"]
                        B.wait("vector", "a", a_si)
                        B.emit("vector", lambda e, hc=hc, par=par:
                               e.tensor_tensor(
                                   out=actb[:, hc * 512:(hc + 1) * 512],
                                   in0=sgs[par][:], in1=pu[par][:],
                                   op=ALU.mult), inc=("v", 1))
                        B.drain("vector")
                        v_act[hc] = B.cnt["v"]
                    v_act_done = B.cnt["v"]
                    B.wait("tensor", "v", v_act_done)

                with ExitStack() as m3ctx:
                    d3s = [m3ctx.enter_context(nc.sbuf_tensor(
                        f"d3s{tt}_{i}", [128, NHC * 128], BF16))
                        for i in range(2)]
                    yst = [m3ctx.enter_context(nc.sbuf_tensor(
                        f"yst{tt}_{i}", [128, 512], F32)) for i in range(2)]
                    ytst = [m3ctx.enter_context(nc.sbuf_tensor(
                        f"ytst{tt}_{i}", [128, 512], F32)) for i in range(2)]

                    t_m3 = {}
                    v_yev = {}
                    t_ytr = {}
                    v_ytev = {}
                    for dc in range(NKC):
                        par = dc % 2
                        if dc >= 2:
                            B.wait("scalar", "t", t_m3[dc - 2])
                        else:
                            B.wait("scalar", "v", v_act_done)
                        B.emit("scalar", lambda e, dc=dc, par=par: e.dma_start(
                            out=d3s[par][:], in_=dq3_d[dc]),
                            inc=(d3_sems[par], 16))
                        d3_tot = B.cnt[d3_sems[par]]
                        B.wait("tensor", d3_sems[par], d3_tot)
                        if dc >= 2:
                            B.wait("tensor", "v", v_yev[dc - 2])
                        for hc in range(NHC):
                            B.emit("tensor", lambda e, hc=hc, par=par: e.matmul(
                                pyf[par][:],
                                lhsT=d3s[par][:, hc * 128:(hc + 1) * 128],
                                rhs=actb[:, hc * 512:(hc + 1) * 512],
                                start=(hc == 0), stop=(hc == NHC - 1)),
                                inc=("t", 1) if hc == NHC - 1 else None)
                        t_m3[dc] = B.cnt["t"]
                        B.wait("vector", "t", t_m3[dc])
                        B.emit("vector", lambda e, par=par: e.tensor_copy(
                            out=yst[par][:], in_=pyf[par][:]), inc=("v", 1))
                        B.drain("vector")
                        v_yev[dc] = B.cnt["v"]
                        B.wait("tensor", "v", v_yev[dc])
                        if dc >= 2:
                            B.wait("tensor", "v", v_ytev[dc - 2])
                        for sj in range(4):
                            B.emit("tensor", lambda e, sj=sj, par=par: e.matmul(
                                pyt[par][:, sj * 128:(sj + 1) * 128],
                                lhsT=yst[par][:, sj * 128:(sj + 1) * 128],
                                rhs=identf[:], start=True, stop=True),
                                inc=("t", 1) if sj == 3 else None)
                        t_ytr[dc] = B.cnt["t"]
                        B.wait("vector", "t", t_ytr[dc])
                        B.wait("vector", gy_sems[par], B.cnt[gy_sems[par]])
                        B.emit("vector", lambda e, par=par: e.tensor_copy(
                            out=ytst[par][:], in_=pyt[par][:]), inc=("v", 1))
                        B.drain("vector")
                        v_ytev[dc] = B.cnt["v"]
                        ydst = ytm_d.rearrange("s p d -> p s d")[
                            :, :, dc * 128:(dc + 1) * 128]
                        B.wait("gpsimd", "v", v_ytev[dc])
                        B.emit("gpsimd", lambda e, ydst=ydst, par=par:
                               e.dma_start(
                                   out=ydst, in_=ytst[par][:].rearrange(
                                       "p (s d) -> p s d", d=128)),
                               inc=(gy_sems[par], 16))
                    gy_a_tot = B.cnt["gy_a"]
                    gy_b_tot = B.cnt["gy_b"]

                B.wait("sync", "gy_a", gy_a_tot)
                B.wait("sync", "gy_b", gy_b_tot)
                x1_sems = ["xi_a", "xi_b"]
                for s in range(4):
                    par = s % 2
                    B.wait("sync", "a")
                    B.emit("sync", lambda e, s=s: e.dma_start(
                        out=ytsb[:], in_=ytm_d[s]), inc=("ye_a", 16))
                    ye0 = B.cnt["ye_a"]
                    B.wait("sync", so_sems[par], B.cnt[so_sems[par]])
                    B.emit("sync", lambda e, tt=tt, s=s, par=par:
                           e.dma_start(
                               out=outb[par][:],
                               in_=x1_d[tt * 512 + s * 128:
                                        tt * 512 + (s + 1) * 128, :]),
                           inc=(x1_sems[par], 16))
                    ye1 = B.cnt[x1_sems[par]]
                    B.wait("vector", "ye_a", ye0)
                    B.wait("scalar", "ye_a", ye0)
                    _ln_emit(B, ytsb[:], st2[:], [ytsb[:]],
                             sqscr=actb[:, 0:D])
                    a_ln3 = B.cnt["a"]
                    B.wait("vector", "a", a_ln3)
                    B.wait("vector", x1_sems[par], ye1)
                    B.emit("vector", lambda e, par=par: e.tensor_add(
                        outb[par][:], outb[par][:], ytsb[:]), inc=("v", 1))
                    B.drain("vector")
                    v_out = B.cnt["v"]
                    if debug and tt == 0 and s == 0:
                        B.wait("sync", "v", v_out)
                        B.emit("sync", lambda e: e.dma_start(
                            out=dbg["yt"], in_=ytsb[:]), inc=("dbgs", 16))
                    B.wait("sync", "v", v_out)
                    B.emit("sync", lambda e, tt=tt, s=s, par=par:
                           e.dma_start(
                               out=out_d[tt * 512 + s * 128:
                                         tt * 512 + (s + 1) * 128, :],
                               in_=outb[par][:]), inc=(so_sems[par], 16))

            for _tt in range(NTILE):
                _p2_tile(_tt)

            B.wait("sync", "so_a", B.cnt["so_a"])
            B.wait("sync", "so_b", B.cnt["so_b"])
            if debug:
                B.wait("sync", "dbgs", B.cnt["dbgs"])

    for nm in ("dqs_a", "dqs_b", "sx1", "sl2_a", "sl2_b", "gv",
               "gs_a", "gs_b", "gs_c", "gs_d", "gy_a", "gy_b", "so_a",
               "so_b", "cc", "gx"):
        B.wait("sync", nm)
    B.finalize_block()
    B.ctx.close()
    return nc


_CACHE = {}


def kernel(L0_attn_x, L0_ffn_w1, L0_ffn_w2, L0_ffn_w3):
    x = np.ascontiguousarray(np.asarray(L0_attn_x, dtype=np.float32)
                             .reshape(16384, D))
    w1 = np.ascontiguousarray(np.asarray(L0_ffn_w1, dtype=np.float32))
    w2 = np.ascontiguousarray(np.asarray(L0_ffn_w2, dtype=np.float32))
    w3 = np.ascontiguousarray(np.asarray(L0_ffn_w3, dtype=np.float32))
    if "consts" not in _CACHE:
        _CACHE["consts"] = _build_consts()
    consts = _CACHE["consts"]
    T = 16384 // NCORES
    nc = build_nc(T=T)
    w1p = np.concatenate([w1, np.zeros((88 * 128 - H, D), np.float32)], 0)
    w2p = np.concatenate([w2, np.zeros((88 * 128 - H, D), np.float32)], 0)
    r1 = (88 // NCORES) * 128
    r3 = (NKC // NCORES) * 128
    in_maps = []
    for i in range(NCORES):
        m = {"x": x[i * T:(i + 1) * T],
             "w1s": np.ascontiguousarray(w1p[i * r1:(i + 1) * r1]),
             "w2s": np.ascontiguousarray(w2p[i * r1:(i + 1) * r1]),
             "w3s": np.ascontiguousarray(w3[i * r3:(i + 1) * r3])}
        m.update(consts)
        in_maps.append(m)
    res = run_bass_kernel_spmd(nc, in_maps, core_ids=list(range(NCORES)))
    out = np.concatenate([res.results[i]["out"] for i in range(NCORES)],
                         axis=0)
    return out.reshape(2, 8192, D)
